# version 10
# baseline (speedup 1.0000x reference)
from collections import defaultdict
from contextlib import ExitStack

import numpy as np

import concourse.bass as bass
import concourse.mybir as mybir
from concourse.bass_utils import run_bass_kernel_spmd

F32 = mybir.dt.float32
BF16 = mybir.dt.bfloat16
FP8 = mybir.dt.float8e4
AF = mybir.ActivationFunctionType
ALU = mybir.AluOpType

T, C, H, HR, D, DH = 4096, 2048, 8, 16, 128, 8192
NCH, NTS, CT, HT = 8, 32, 16, 64
OFF = 6.906768
LNS = float(np.log(1.0 / np.sqrt(D)))
EPS_LN = 1e-5


class Rec:

    ENGS = ("t", "v", "s", "sp", "pl")

    def __init__(self, nc, semstack, sems):
        self.nc = nc
        self.ops = {e: [] for e in self.ENGS}
        self.sems = sems
        self.semstack = semstack
        self.cnt = defaultdict(int)

    def sem(self, name):
        if name not in self.sems:
            self.sems[name] = self.semstack.enter_context(self.nc.semaphore(name))
        return self.sems[name]

    def op(self, eng, fn, waits=(), inc=None, inck=1):
        w = [(n, v) for (n, v) in waits if v > 0]
        for n, _ in w:
            self.sem(n)
        if inc is not None:
            self.sem(inc)
            self.cnt[inc] += inck
        self.ops[eng].append((fn, w, inc, inck))
        return self.cnt[inc] if inc is not None else None

    def replay(self, eng_name, eng):
        done = defaultdict(int)
        for fn, waits, inc, inck in self.ops[eng_name]:
            for n, v in waits:
                if v > done[n]:
                    eng.wait_ge(self.sems[n], v)
                    done[n] = v
            instr = fn(eng)
            if inc is not None:
                instr.then_inc(self.sems[inc], inck)

    def emit_block(self):
        with self.nc.Block() as blk:
            @blk.tensor
            def _(t):
                self.replay("t", t)

            @blk.vector
            def _(v):
                self.replay("v", v)

            @blk.scalar
            def _(s):
                self.replay("s", s)

            @blk.sync
            def _(sp):
                self.replay("sp", sp)

            @blk.gpsimd
            def _(pl):
                self.replay("pl", pl)


def build(upto=6, dbg_out=()):
    nc = bass.Bass("TRN2", target_bir_lowering=False)
    dbg_out = set(dbg_out)

    def dram(name, shape, dtype, kind="Internal"):
        k = "ExternalOutput" if name in dbg_out else kind
        return nc.dram_tensor(name, shape, dtype, kind=k).ap()

    xT = dram("xT", [C, T], F32, "ExternalInput")
    x_in = dram("x", [T, C], F32, "ExternalInput")
    Wq = dram("Wq", [C, 2048], F32, "ExternalInput")
    Wk = dram("Wk", [C, 1024], F32, "ExternalInput")
    Wv = dram("Wv", [C, 1024], F32, "ExternalInput")
    WgR = dram("WgR", [128, 128], F32, "ExternalInput")
    Wc = dram("Wc", [2048, C], F32, "ExternalInput")
    bc = dram("bc", [1, C], F32, "ExternalInput")
    upT = dram("upT", [C, DH], F32, "ExternalInput")
    dnT = dram("dnT", [DH, C], F32, "ExternalInput")
    upsc = dram("upsc", [128, 64], F32, "ExternalInput")
    dsc = dram("dsc", [1, C], F32, "ExternalInput")
    osc = dram("osc", [1, 1], F32, "ExternalInput")
    sel = dram("sel", [8, 1024], F32, "ExternalInput")
    tri = dram("tri", [128, 128], F32, "ExternalInput")
    ident = dram("ident", [128, 128], F32, "ExternalInput")
    onesc = dram("onesc", [1, 128], F32, "ExternalInput")
    out = dram("out", [T, C], F32, "ExternalOutput")

    qS = dram("qS", [HR, 128, T], BF16)
    kS = dram("kS", [H, 128, T], BF16)
    vS = dram("vS", [H, 128, 33 * 128], BF16)
    yTS = dram("yTS", [HR, 128, T], BF16)
    pre1S = dram("pre1S", [T, C], BF16)
    x1S = dram("x1S", [T, C], BF16)
    hidS = dram("hidS", [HT, 128, T], BF16)
    pre2S = dram("pre2S", [T, C], BF16)

    big = ExitStack()
    sems = {}
    with big:
        onescol = big.enter_context(nc.sbuf_tensor("onescol", [1, 128], F32))
        tri_bf = big.enter_context(nc.sbuf_tensor("tri_bf", [128, 128], BF16))
        sel_bf = big.enter_context(nc.sbuf_tensor("sel_bf", [8, 1024], BF16))
        id_sb = big.enter_context(nc.sbuf_tensor("id_sb", [128, 8], F32))
        cbias = big.enter_context(nc.sbuf_tensor("cbias", [128, 4], F32))
        mp1 = big.enter_context(nc.sbuf_tensor("mp1", [128, 128], F32))
        sq1 = big.enter_context(nc.sbuf_tensor("sq1", [128, 128], F32))
        m1a = big.enter_context(nc.sbuf_tensor("m1a", [128, 32], F32))
        v1a = big.enter_context(nc.sbuf_tensor("v1a", [128, 32], F32))
        r1a = big.enter_context(nc.sbuf_tensor("r1a", [128, 32], F32))
        m2p = big.enter_context(nc.sbuf_tensor("m2p", [128, 128], F32))
        s2p = big.enter_context(nc.sbuf_tensor("s2p", [128, 128], F32))
        m2a = big.enter_context(nc.sbuf_tensor("m2a", [128, 32], F32))
        v2a = big.enter_context(nc.sbuf_tensor("v2a", [128, 32], F32))
        r2a = big.enter_context(nc.sbuf_tensor("r2a", [128, 32], F32))

        with ExitStack() as a_arena:
            xT_sb = a_arena.enter_context(nc.sbuf_tensor("xT_sb", [128, CT * T], BF16))
            expq = a_arena.enter_context(nc.sbuf_tensor("expq", [8, T], BF16))
            col_e = a_arena.enter_context(nc.sbuf_tensor("col_e", [128, 8 * NTS], F32))
            with ExitStack() as p1:
                wg_bf = p1.enter_context(nc.sbuf_tensor("wg_bf", [128, 128], BF16))
                gstk = p1.enter_context(nc.sbuf_tensor("gstk", [128, 2 * T], F32))
                gP = gstk[0:8, 0:T]
                gQ = gstk[0:8, T:2 * T]
                gR = gstk[32:40, 0:T]
                gS = gstk[32:40, T:2 * T]
                bcH = p1.enter_context(nc.sbuf_tensor("bcH", [128, T], BF16))
                w_bf = [p1.enter_context(nc.sbuf_tensor(f"w_bf{i}", [128, 2048], BF16)) for i in range(2)]
                q_st = [p1.enter_context(nc.sbuf_tensor(f"q_st{i}", [128, T], BF16)) for i in range(2)]
                ps_g = p1.enter_context(nc.psum_tensor("ps_g", [8, 512], F32))
                ps_ct = [p1.enter_context(nc.psum_tensor(f"ps_ct{i}", [128, 8], F32)) for i in range(2)]
                ps_q = [p1.enter_context(nc.psum_tensor(f"ps_q{i}", [128, 512], F32)) for i in range(2)]
                ps_bc = [p1.enter_context(nc.psum_tensor(f"ps_bc{i}", [128, 512], F32)) for i in range(2)]

                r = Rec(nc, big, sems)
                for ap_in, ap_out in ((onesc, onescol[:, :]),
                                      (ident[0:8, 0:8], id_sb[0:8, :])):
                    r.op("sp", lambda e, a=ap_in, o=ap_out: e.dma_start(out=o, in_=a),
                         inc="ldc", inck=16)
                nldc = r.cnt["ldc"]
                for ci, val in ((0, OFF), (1, -OFF), (2, LNS), (3, EPS_LN)):
                    r.op("v", lambda e, ci=ci, val=val: e.memset(cbias[:, ci:ci + 1], val),
                         inc="cb")
                ncb = r.cnt["cb"]
                r.op("pl", lambda e: e.dma_start(out=wg_bf[:, :], in_=WgR), inc="plk", inck=16)
                r.op("pl", lambda e: e.dma_start(out=sel_bf[:, :], in_=sel), inc="plk", inck=16)
                r.op("pl", lambda e: e.dma_start(out=tri_bf[:, :], in_=tri), inc="plk", inck=16)
                for ct in range(CT):
                    r.op("pl", lambda e, ct=ct: e.dma_start(
                        out=xT_sb[:, T * ct:T * (ct + 1)], in_=xT[128 * ct:128 * (ct + 1), :]),
                        inc="plx", inck=16)
                nplx_all = r.cnt["plx"]

                for i in range(NCH):
                    for ct in range(CT):
                        r.op("t", lambda e, i=i, ct=ct: e.matmul(
                            ps_g[:, :], wg_bf[:, 8 * ct:8 * ct + 8],
                            xT_sb[:, T * ct + 512 * i: T * ct + 512 * (i + 1)],
                            start=(ct == 0), stop=(ct == CT - 1)),
                            waits=[("plx", nplx_all), ("plk", 48), ("ldc", nldc),
                                   ("sg", r.cnt["sg"])], inc="tg")
                    r.op("s", lambda e, i=i: e.copy(gP[:, 512 * i:512 * (i + 1)], ps_g[:, :]),
                         waits=[("tg", r.cnt["tg"])], inc="sg")
                nsg = r.cnt["sg"]
                r.op("s", lambda e: e.activation(gQ, gP, AF.Abs, bias=cbias[0:8, 0:1]),
                     waits=[("sg", nsg), ("cb", ncb)], inc="sg")
                r.op("s", lambda e: e.activation(gR, gQ, AF.Exp, scale=-1.0),
                     waits=[("sg", r.cnt["sg"])], inc="sg")
                r.op("s", lambda e: e.activation(gS, gR, AF.Ln, bias=1.0),
                     waits=[("sg", r.cnt["sg"])], inc="sg")
                r.op("s", lambda e: e.activation(gR, gP, AF.Relu, scale=-1.0,
                                                 bias=cbias[0:8, 1:2]),
                     waits=[("sg", r.cnt["sg"])], inc="sg")
                r.op("v", lambda e: e.tensor_tensor_scan(gQ, gS, gR, 0.0,
                                                         op0=ALU.add, op1=ALU.add),
                     waits=[("sg", r.cnt["sg"])], inc="vm")
                nscan = r.cnt["vm"]
                r.op("s", lambda e: e.activation(expq[:, :], gQ, AF.Exp, scale=-0.5,
                                                 bias=cbias[0:8, 2:3]),
                     waits=[("vm", nscan)], inc="sg")
                nexpq = r.cnt["sg"]
                for ts in range(NTS):
                    r.op("t", lambda e, ts=ts: e.transpose(
                        ps_ct[ts % 2][:, :], gstk[0:8, T + 128 * ts:T + 128 * (ts + 1)],
                        id_sb[0:8, :]),
                        waits=[("vm", nscan), ("sce", max(0, r.cnt["sce"] - 1))], inc="tct")
                    r.op("s", lambda e, ts=ts: e.activation(
                        col_e[:, 8 * ts:8 * (ts + 1)], ps_ct[ts % 2][:, :], AF.Exp),
                        waits=[("tct", r.cnt["tct"])], inc="sce")

                for u in range(HR + H):
                    is_q = u < HR
                    w_src = Wq if is_q else Wk
                    col0 = 128 * u if is_q else 128 * (u - HR)
                    buf = u % 2
                    for half in range(2):
                        r.op("pl", lambda e, w=w_src, c0=col0, hh=half, b=buf: e.dma_start(
                            out=w_bf[b][:, 1024 * hh:1024 * (hh + 1)].rearrange(
                                "p (ct d) -> p ct d", d=128),
                            in_=w[1024 * hh:1024 * (hh + 1), c0:c0 + 128].rearrange(
                                "(ct p) d -> p ct d", p=128)),
                            waits=[("tp", max(0, 128 * (u - 1)))],
                            inc=f"plw{buf}", inck=16)
                    if is_q and u % 2 == 0:
                        head = u // 2
                        for i in range(NCH):
                            r.op("t", lambda e, hd=head, i=i: e.matmul(
                                ps_bc[i % 2][:, :], sel_bf[0:8, 128 * hd:128 * (hd + 1)],
                                expq[0:8, 512 * i:512 * (i + 1)], start=True, stop=True),
                                waits=[("sg", nexpq), ("plk", 48),
                                       ("sbc", max(0, r.cnt["sbc"] - 1)),
                                       ("vq", 8 * u)], inc="tbc")
                            r.op("s", lambda e, i=i: e.copy(bcH[:, 512 * i:512 * (i + 1)],
                                                            ps_bc[i % 2][:, :]),
                                 waits=[("tbc", r.cnt["tbc"])], inc="sbc")
                    for i in range(NCH):
                        gq = NCH * u + i
                        for ct in range(CT):
                            r.op("t", lambda e, b=buf, ct=ct, i=i: e.matmul(
                                ps_q[i % 2][:, :], w_bf[b][:, 128 * ct:128 * (ct + 1)],
                                xT_sb[:, T * ct + 512 * i:T * ct + 512 * (i + 1)],
                                start=(ct == 0), stop=(ct == CT - 1)),
                                waits=[(f"plw{u % 2}", 32 * (u // 2 + 1)),
                                       ("vq", max(0, gq - 1))],
                                inc="tp")
                        if is_q:
                            r.op("v", lambda e, b=buf, i=i: e.tensor_tensor(
                                q_st[b][:, 512 * i:512 * (i + 1)], ps_q[i % 2][:, :],
                                bcH[:, 512 * i:512 * (i + 1)], op=ALU.mult),
                                waits=[("tp", r.cnt["tp"]), ("sbc", 8 * (u // 2 + 1)),
                                       (f"spq{u % 2}", max(0, 16 * (u // 2)))], inc="vq")
                        else:
                            r.op("v", lambda e, b=buf, i=i: e.tensor_copy(
                                q_st[b][:, 512 * i:512 * (i + 1)], ps_q[i % 2][:, :]),
                                waits=[("tp", r.cnt["tp"]),
                                       (f"spq{u % 2}", max(0, 16 * (u // 2)))],
                                inc="vq")
                    dst = qS[u] if is_q else kS[u - HR]
                    r.op("sp", lambda e, b=buf, d=dst: e.dma_start(out=d, in_=q_st[b][:, :]),
                         waits=[("vq", r.cnt["vq"])], inc=f"spq{buf}", inck=16)
                r.op("sp", lambda e: e.nop(), waits=[("spq0", r.cnt["spq0"]),
                                                     ("spq1", r.cnt["spq1"])])
                r.emit_block()

            with ExitStack() as p1b:
                wv_bf = p1b.enter_context(nc.sbuf_tensor("wv_bf", [128, CT * 512], BF16))
                vv_st = p1b.enter_context(nc.sbuf_tensor("vv_st", [128, 4 * 33 * 128], BF16))
                ps_v = [p1b.enter_context(nc.psum_tensor(f"ps_v{i}", [128, 512], F32)) for i in range(2)]
                r = Rec(nc, big, sems)
                VW = 33 * 128
                for hg in range(2):
                    for half in range(2):
                        r.op("pl", lambda e, hg=hg, hh=half: e.dma_start(
                            out=wv_bf[:, CT * 256 * hh:CT * 256 * (hh + 1)].rearrange(
                                "p (ct d) -> p ct d", d=512),
                            in_=Wv[1024 * hh:1024 * (hh + 1), 512 * hg:512 * (hg + 1)].rearrange(
                                "(ct p) d -> p ct d", p=128)),
                            waits=[("tv", max(0, 512 * (2 * hg - 1)))], inc="plv", inck=16)
                    r.op("v", lambda e: e.memset(vv_st[:, :], 0.0),
                         waits=[("spv", 64 * hg)], inc="vv")
                    n_ms = r.cnt["vv"]
                    for ts in range(NTS):
                        for ct in range(CT):
                            r.op("t", lambda e, ts=ts, ct=ct: e.matmul(
                                ps_v[ts % 2][:, :],
                                xT_sb[:, T * ct + 128 * ts:T * ct + 128 * (ts + 1)],
                                wv_bf[:, 512 * ct:512 * (ct + 1)],
                                start=(ct == 0), stop=(ct == CT - 1)),
                                waits=[("plv", 32 * (hg + 1)),
                                       ("vv", max(0, r.cnt["vv"] - 8))], inc="tv")
                        for hl in range(4):
                            head = 4 * hg + hl
                            r.op("v", lambda e, ts=ts, hl=hl, hd=head: e.tensor_scalar(
                                vv_st[:, VW * hl + 132 * ts:VW * hl + 132 * ts + 128],
                                ps_v[ts % 2][:, 128 * hl:128 * (hl + 1)],
                                col_e[:, 8 * ts + hd:8 * ts + hd + 1], None, op0=ALU.mult),
                                waits=[("tv", r.cnt["tv"]), ("vv", n_ms)], inc="vv")
                            r.op("v", lambda e, ts=ts, hl=hl, hd=head: e.tensor_copy(
                                vv_st[:, VW * hl + 132 * ts + 128:VW * hl + 132 * ts + 129],
                                col_e[:, 8 * ts + hd:8 * ts + hd + 1]),
                                inc="vv")
                    for hl in range(4):
                        r.op("sp", lambda e, hg=hg, hl=hl: e.dma_start(
                            out=vS[4 * hg + hl], in_=vv_st[:, VW * hl:VW * (hl + 1)]),
                            waits=[("vv", r.cnt["vv"])], inc="spv", inck=16)
                r.op("sp", lambda e: e.nop(), waits=[("spv", r.cnt["spv"])])
                r.emit_block()

        if upto < 2:
            return nc

        with ExitStack() as p2:
            qA = [p2.enter_context(nc.sbuf_tensor(f"qA{i}", [128, T], BF16)) for i in range(2)]
            qB = [p2.enter_context(nc.sbuf_tensor(f"qB{i}", [128, T], BF16)) for i in range(2)]
            kk = [p2.enter_context(nc.sbuf_tensor(f"kk{i}", [128, T], BF16)) for i in range(2)]
            vv = [p2.enter_context(nc.sbuf_tensor(f"vv{i}", [128, 33 * 128], BF16)) for i in range(2)]
            A_buf = [p2.enter_context(nc.sbuf_tensor(f"A_buf{i}", [128, 32 * 512], BF16)) for i in range(2)]
            y_sb = [p2.enter_context(nc.sbuf_tensor(f"y_sb{i}", [128, T], BF16)) for i in range(2)]
            yT_sb = [p2.enter_context(nc.sbuf_tensor(f"yT_sb{i}", [128, T], BF16)) for i in range(2)]
            vsq = p2.enter_context(nc.sbuf_tensor("vsq", [128, 1024], F32))
            ncol = p2.enter_context(nc.sbuf_tensor("ncol", [128, 1], F32))
            rcol = p2.enter_context(nc.sbuf_tensor("rcol", [128, 1], F32))
            psS = [p2.enter_context(nc.psum_tensor(f"psS{i}", [128, 1024], F32)) for i in range(3)]
            psY = [p2.enter_context(nc.psum_tensor(f"psY{i}", [128, 129], F32)) for i in range(2)]
            r = Rec(nc, big, sems)

            NPS = 3
            gp = [0]
            pair_free = {}
            G = [0]

            def emit_S_and_squares(h, sig, i, slot):
                q_sb = qA[slot] if sig == 0 else qB[slot]
                nj = 4 * i + 4
                npair = 2 * i
                avbase = r.cnt["avT"]
                pair_eng = ["v" if (p % 3 == 2) else "s" for p in range(npair)]
                tile_done = {}
                cnt_s = r.cnt["sq"]
                cnt_vm = r.cnt["sqv"]
                cnt_vc = r.cnt["sqc"]
                my_gp = {}
                for p in range(npair + 2):
                    my_gp[p] = gp[0]
                    gp[0] += 1
                for p in range(npair):
                    if pair_eng[p] == "v":
                        cnt_vc += 1
                        cnt_vm += 1
                        pair_free[my_gp[p]] = ("sqc", cnt_vc)
                        tile_done[2 * p] = ("sqv", cnt_vm)
                        tile_done[2 * p + 1] = ("sqv", cnt_vm)
                    else:
                        cnt_s += 1
                        pair_free[my_gp[p]] = ("sq", cnt_s)
                        tile_done[2 * p] = ("sq", cnt_s)
                        tile_done[2 * p + 1] = ("sq", cnt_s)
                for rr in range(4):
                    cnt_s += 1
                    j = 4 * i + rr
                    pair_free[my_gp[j // 2]] = ("sq", cnt_s)
                    tile_done[j] = ("sq", cnt_s)
                sbase = r.cnt["sS"]
                for j in range(nj):
                    p = j // 2
                    a = my_gp[p] % NPS
                    cslot = j % 2
                    rr = j - 4 * i
                    c0 = 128 * rr if rr >= 0 else 0
                    w = []
                    if my_gp[p] - NPS in pair_free:
                        w.append(pair_free[my_gp[p] - NPS])
                    if j == 0 and i == 0 and sig == 0:
                        w.append((f"ld2{slot}", 64 * (h // 2 + 1)))
                    r.op("t", lambda e, a=a, cs=cslot, j=j, c0=c0, i=i, q=q_sb,
                         ks=kk[slot]: e.matmul(
                             psS[a][:, 512 * cs + c0:512 * (cs + 1)],
                             ks[:, 128 * j:128 * (j + 1)],
                             q[:, 512 * i + c0:512 * (i + 1)],
                             start=True, stop=True),
                         waits=w, inc="sS")
                for p in range(npair):
                    a = my_gp[p] % NPS
                    if pair_eng[p] == "v":
                        r.op("v", lambda e, a=a: e.tensor_copy(
                            vsq[:, :], psS[a][:, :]),
                            waits=[("sS", sbase + 2 * p + 2), ("avT", avbase),
                                   ("sqv", r.cnt["sqv"])],
                            inc="sqc")
                        r.op("v", lambda e, p=p, sg=sig: e.tensor_tensor(
                            A_buf[sg][:, 1024 * p:1024 * (p + 1)], vsq[:, :],
                            vsq[:, :], op=ALU.mult),
                            waits=[("sqc", r.cnt["sqc"]), ("avT", avbase)],
                            inc="sqv")
                    else:
                        r.op("s", lambda e, a=a, p=p, sg=sig: e.activation(
                            A_buf[sg][:, 1024 * p:1024 * (p + 1)], psS[a][:, :],
                            AF.Square),
                            waits=[("sS", sbase + 2 * p + 2), ("avT", avbase)],
                            inc="sq")
                for rr in range(4):
                    j = 4 * i + rr
                    a = my_gp[j // 2] % NPS
                    cslot = j % 2
                    c0 = 128 * rr
                    r.op("s", lambda e, a=a, cs=cslot, j=j, c0=c0, sg=sig: e.activation(
                        A_buf[sg][:, 512 * j + c0:512 * (j + 1)],
                        psS[a][:, 512 * cs + c0:512 * (cs + 1)], AF.Square),
                        waits=[("sS", sbase + j + 1), ("avT", avbase)], inc="sq")
                for rr in range(4):
                    j = 4 * i + rr
                    r.op("v", lambda e, j=j, rr=rr, sg=sig: e.tensor_tensor(
                        A_buf[sg][:, 512 * j + 128 * rr:512 * j + 128 * (rr + 1)],
                        A_buf[sg][:, 512 * j + 128 * rr:512 * j + 128 * (rr + 1)],
                        tri_bf[:, :], op=ALU.mult),
                        waits=[tile_done[j]], inc="msk")
                mskbase = r.cnt["msk"] - 4
                return tile_done, mskbase

            def emit_AV(h, sig, i, slot, u, tile_done, mskbase):
                for sub in range(4):
                    Gi = G[0]
                    G[0] += 1
                    b = Gi % 2
                    njs = 4 * i + sub + 1
                    for j in range(njs):
                        w = [("ye", max(0, 2 * (Gi - 1)))]
                        if j % 4 == 0:
                            w.append(tile_done[min(j + 3, njs - 1)])
                        if j == 4 * i + sub:
                            w.append(("msk", mskbase + sub + 1))
                        r.op("t", lambda e, b=b, j=j, sub=sub, vs=vv[slot], njs=njs,
                             sg=sig: e.matmul(
                                 psY[b][:, :],
                                 A_buf[sg][:, 512 * j + 128 * sub:512 * j + 128 * (sub + 1)],
                                 vs[:, 132 * j:132 * j + 129],
                                 start=(j == 0), stop=(j == njs - 1)),
                             waits=w, inc="avT")
                    ts_g = 4 * i + sub
                    r.op("s", lambda e, b=b: e.activation(
                        ncol[:, :], psY[b][:, 128:129], AF.Copy, bias=1e-6),
                        waits=[("avT", r.cnt["avT"]), ("ye", max(0, r.cnt["ye"] - 1))],
                        inc="yes")
                    r.op("v", lambda e: e.reciprocal(rcol[:, :], ncol[:, :]),
                         waits=[("yes", r.cnt["yes"]), ("ye", r.cnt["ye"])], inc="ye")
                    r.op("v", lambda e, b=b, sg=sig, ts_g=ts_g: e.tensor_scalar(
                        y_sb[sg][:, 128 * ts_g:128 * (ts_g + 1)], psY[b][:, 0:128],
                        rcol[:, :], None, op0=ALU.mult),
                        waits=[("ye", r.cnt["ye"]),
                               (f"ytp{sig}", r.cnt[f"ytp{sig}"])], inc="ye")

            for h in range(H):
                wait_av = 528 * (2 * h - 2) if h >= 2 else 0
                slot = h % 2
                for src_, dst in ((qS[2 * h], qA[slot][:, :]), (qS[2 * h + 1], qB[slot][:, :]),
                                  (kS[h], kk[slot][:, :]), (vS[h], vv[slot][:, :])):
                    r.op("pl", lambda e, s=src_, d=dst: e.dma_start(out=d, in_=s),
                         waits=[("avT", wait_av)], inc=f"ld2{slot}", inck=16)
                for i in range(NCH):
                    plans = {}
                    for sig in range(2):
                        plans[sig] = emit_S_and_squares(h, sig, i, slot)
                    for sig in range(2):
                        td, mb = plans[sig]
                        emit_AV(h, sig, i, slot, 2 * h + sig, td, mb)
                for sig in range(2):
                    u = 2 * h + sig
                    ye_all = r.cnt["ye"]
                    ntp = r.op("sp", lambda e, sg=sig: e.dma_start(
                        out=yT_sb[sg][:, :].rearrange("p (n f) -> p n f", f=128),
                        in_=y_sb[sg][:, :].rearrange("p (n f) -> p n f", f=128),
                        transpose=True),
                        waits=[("ye", ye_all), (f"ysp{sig}", r.cnt[f"ysp{sig}"])],
                        inc=f"ytp{sig}", inck=16)
                    r.op("sp", lambda e, u=u, sg=sig: e.dma_start(
                        out=yTS[u], in_=yT_sb[sg][:, :]),
                        waits=[(f"ytp{sig}", ntp)], inc=f"ysp{sig}", inck=16)
            r.op("sp", lambda e: e.nop(), waits=[("ysp0", r.cnt["ysp0"]),
                                                 ("ysp1", r.cnt["ysp1"])])
            r.emit_block()

        if upto < 3:
            return nc

        with ExitStack() as p3:
            bc_sb = p3.enter_context(nc.sbuf_tensor("bc_sb", [1, C], F32))
            bcbc = p3.enter_context(nc.sbuf_tensor("bcbc", [128, C], F32))
            wc_bf = p3.enter_context(nc.sbuf_tensor("wc_bf", [128, 16 * 2048], BF16))
            yT_ld = [p3.enter_context(nc.sbuf_tensor(f"yT_ld{i}", [128, 2048], BF16)) for i in range(3)]
            x_ld = [p3.enter_context(nc.sbuf_tensor(f"x_ld{i}", [128, 2048], F32)) for i in range(3)]
            t1_sb = [p3.enter_context(nc.sbuf_tensor(f"t1_sb{i}", [128, 512], F32)) for i in range(2)]
            pre1_st = [p3.enter_context(nc.sbuf_tensor(f"pre1_st{i}", [128, 2048], BF16)) for i in range(2)]
            trash = p3.enter_context(nc.sbuf_tensor("trash", [128, 512], BF16))
            ps_att = [p3.enter_context(nc.psum_tensor(f"ps_att{i}", [128, 512], F32)) for i in range(2)]
            r = Rec(nc, big, sems)
            for s in range(16):
                r.op("pl", lambda e, s=s: e.dma_start(
                    out=wc_bf[:, 2048 * s:2048 * (s + 1)], in_=Wc[128 * s:128 * (s + 1), :]),
                    inc="plc", inck=16)
            r.op("sp", lambda e: e.dma_start(out=bc_sb[:, :], in_=bc), inc="lb", inck=16)
            for c in range(4):
                r.op("t", lambda e, c=c: e.matmul(
                    ps_att[0][:, :], onescol[:, :], bc_sb[0:1, 512 * c:512 * (c + 1)],
                    start=True, stop=True),
                    waits=[("lb", 16), ("s3b", r.cnt["s3b"])], inc="t3b")
                r.op("s", lambda e, c=c: e.copy(bcbc[:, 512 * c:512 * (c + 1)],
                                                ps_att[0][:, :]),
                     waits=[("t3b", r.cnt["t3b"])], inc="s3b")
            ns3b = r.cnt["s3b"]
            for ts in range(NTS):
                sl = ts % 3
                pb2 = ts % 2
                wv = max(0, 12 * (ts - 2))
                for g in range(2):
                    r.op("sp", lambda e, ts=ts, sl=sl, g=g: e.dma_start(
                        out=yT_ld[sl][:, 1024 * g:1024 * (g + 1)].rearrange(
                            "p (s t) -> p s t", t=128),
                        in_=yTS[8 * g:8 * (g + 1), :, 128 * ts:128 * (ts + 1)].rearrange(
                            "s p t -> p s t")),
                        waits=[("p3v", wv)], inc=f"l3a{sl}", inck=16)
                r.op("pl", lambda e, ts=ts, sl=sl: e.dma_start(
                    out=x_ld[sl][:, :], in_=x_in[128 * ts:128 * (ts + 1), :]),
                    waits=[("p3v", wv)], inc=f"l3{sl}", inck=16)
                nl3 = r.cnt[f"l3{sl}"]
                nl3a = r.cnt[f"l3a{sl}"]
                for c in range(4):
                    gi = 4 * ts + c
                    pb = c % 2
                    for s in range(16):
                        w = [("plc", 256), ("s3b", ns3b)]
                        if s == 0:
                            w += [(f"l3{sl}", nl3), (f"l3a{sl}", nl3a),
                                  ("p3v", max(0, 3 * (gi - 1)))]
                        r.op("t", lambda e, pb=pb, sl=sl, s=s, c=c: e.matmul(
                            ps_att[pb][:, :], yT_ld[sl][:, 128 * s:128 * (s + 1)],
                            wc_bf[:, 2048 * s + 512 * c:2048 * s + 512 * (c + 1)],
                            start=(s == 0), stop=(s == 15)),
                            waits=w, inc="t3")
                    r.op("v", lambda e, pb=pb, c=c: e.tensor_tensor(
                        t1_sb[pb][:, :], ps_att[pb][:, :],
                        bcbc[:, 512 * c:512 * (c + 1)], op=ALU.add),
                        waits=[("t3", r.cnt["t3"])], inc="p3v")
                    r.op("v", lambda e, pb=pb, sl=sl, pb2=pb2, c=c: e.tensor_tensor(
                        pre1_st[pb2][:, 512 * c:512 * (c + 1)], t1_sb[pb][:, :],
                        x_ld[sl][:, 512 * c:512 * (c + 1)], op=ALU.add),
                        waits=[("p3v", r.cnt["p3v"]), (f"sp1{pb2}", r.cnt[f"sp1{pb2}"]),
                               ("p3s", max(0, 4 * (ts - 1)))], inc="p3v")
                    r.op("v", lambda e, pb2=pb2, ts=ts, c=c: e.reduce_sum(
                        mp1[:, 4 * ts + c:4 * ts + c + 1],
                        pre1_st[pb2][:, 512 * c:512 * (c + 1)], axis=mybir.AxisListType.X),
                        waits=[("p3v", r.cnt["p3v"])], inc="p3v")
                    r.op("s", lambda e, pb2=pb2, ts=ts, c=c: e.activation(
                        trash[:, :], pre1_st[pb2][:, 512 * c:512 * (c + 1)], AF.Square,
                        accum_out=sq1[:, 4 * ts + c:4 * ts + c + 1]),
                        waits=[("p3v", r.cnt["p3v"]), ("p3s", r.cnt["p3s"])], inc="p3s")
                r.op("s", lambda e, ts=ts, pb2=pb2: e.dma_start(
                    out=pre1S[128 * ts:128 * (ts + 1), :], in_=pre1_st[pb2][:, :]),
                    waits=[("p3v", r.cnt["p3v"])], inc=f"sp1{pb2}", inck=16)
            r.op("sp", lambda e: e.nop(), waits=[("sp10", r.cnt["sp10"]),
                                                 ("sp11", r.cnt["sp11"]),
                                                 ("p3s", r.cnt["p3s"])])
            r.emit_block()

        if upto < 4:
            return nc

        x1T_arena = ExitStack()
        x1T_sb = x1T_arena.enter_context(nc.sbuf_tensor("x1T_sb", [128, CT * T], BF16))
        with ExitStack() as p4:
            pre1_ld = [p4.enter_context(nc.sbuf_tensor(f"pre1_ld{i}", [128, 2048], BF16)) for i in range(2)]
            x1_st = [p4.enter_context(nc.sbuf_tensor(f"x1_st{i}", [128, 2048], BF16)) for i in range(2)]
            up_bf = [p4.enter_context(nc.sbuf_tensor(f"up_bf{i}", [128, 2048], BF16)) for i in range(2)]
            up_s8 = [p4.enter_context(nc.sbuf_tensor(f"up_s8{i}", [128, 2048], FP8)) for i in range(2)]
            hid_st = [p4.enter_context(nc.sbuf_tensor(f"hid_st{i}", [128, T], BF16)) for i in range(2)]
            upsc_sb = p4.enter_context(nc.sbuf_tensor("upsc_sb", [128, 64], F32))
            ps_h = [p4.enter_context(nc.psum_tensor(f"ps_h{i}", [128, 512], F32)) for i in range(2)]
            r = Rec(nc, big, sems)
            for ts in range(NTS):
                r.op("v", lambda e, ts=ts: e.reduce_sum(
                    m1a[:, ts:ts + 1], mp1[:, 4 * ts:4 * (ts + 1)], axis=mybir.AxisListType.X),
                    inc="bv")
                r.op("v", lambda e, ts=ts: e.reduce_sum(
                    v1a[:, ts:ts + 1], sq1[:, 4 * ts:4 * (ts + 1)], axis=mybir.AxisListType.X),
                    inc="bv")
            nred = r.cnt["bv"]
            r.op("v", lambda e: e.tensor_scalar(m1a[:, :], m1a[:, :], 1.0 / C, None, op0=ALU.mult),
                 waits=[("bv", nred)], inc="bv")
            r.op("v", lambda e: e.tensor_scalar(v1a[:, :], v1a[:, :], 1.0 / C, None, op0=ALU.mult),
                 waits=[("bv", r.cnt["bv"])], inc="bv")
            r.op("v", lambda e: e.tensor_tensor(r1a[:, :], m1a[:, :], m1a[:, :], op=ALU.mult),
                 waits=[("bv", r.cnt["bv"])], inc="bv")
            r.op("v", lambda e: e.tensor_tensor(v1a[:, :], v1a[:, :], r1a[:, :], op=ALU.subtract),
                 waits=[("bv", r.cnt["bv"])], inc="bv")
            r.op("s", lambda e: e.activation(r1a[:, :], v1a[:, :], AF.Ln, bias=cbias[:, 3:4]),
                 waits=[("bv", r.cnt["bv"])], inc="bs")
            r.op("s", lambda e: e.activation(r1a[:, :], r1a[:, :], AF.Exp, scale=-0.5),
                 waits=[("bs", r.cnt["bs"])], inc="bs")
            nrs = r.cnt["bs"]
            r.op("sp", lambda e: e.dma_start(out=upsc_sb[:, :], in_=upsc), inc="lu", inck=16)
            for ts in range(NTS):
                sl = ts % 2
                wx1t = r.cnt[f"x1t{sl}"]
                wx1s = r.cnt[f"x1s{sl}"]
                r.op("pl", lambda e, ts=ts, sl=sl: e.dma_start(
                    out=pre1_ld[sl][:, :], in_=pre1S[128 * ts:128 * (ts + 1), :]),
                    waits=[("nv", max(0, 4 * (ts - 1)))], inc=f"l3b{sl}", inck=16)
                nl = r.cnt[f"l3b{sl}"]
                for c in range(4):
                    r.op("v", lambda e, sl=sl, ts=ts, c=c: e.tensor_scalar(
                        x1_st[sl][:, 512 * c:512 * (c + 1)],
                        pre1_ld[sl][:, 512 * c:512 * (c + 1)],
                        m1a[:, ts:ts + 1], r1a[:, ts:ts + 1],
                        op0=ALU.subtract, op1=ALU.mult),
                        waits=[(f"l3b{sl}", nl), ("bs", nrs),
                               (f"x1t{sl}", wx1t), (f"x1s{sl}", wx1s)], inc="nv")
                nv_now = r.cnt["nv"]
                r.op("s", lambda e, ts=ts, sl=sl: e.dma_start(
                    out=x1S[128 * ts:128 * (ts + 1), :], in_=x1_st[sl][:, :]),
                    waits=[("nv", nv_now)], inc=f"x1s{sl}", inck=16)
                r.op("sp", lambda e, ts=ts, sl=sl: e.dma_start(
                    out=x1T_sb[:, :].rearrange("p (k tt) -> p k tt", tt=T)[
                        :, :, 128 * ts:128 * (ts + 1)],
                    in_=x1_st[sl][:, :].rearrange("p (k f) -> p k f", f=128),
                    transpose=True),
                    waits=[("nv", nv_now)], inc=f"x1t{sl}", inck=16)
            for ht in range(HT):
                sl = ht % 2
                for half in range(2):
                    r.op("pl", lambda e, ht=ht, sl=sl, hh=half: e.dma_start(
                        out=up_bf[sl][:, 1024 * hh:1024 * (hh + 1)].rearrange(
                            "p (ct d) -> p ct d", d=128),
                        in_=upT[1024 * hh:1024 * (hh + 1), 128 * ht:128 * (ht + 1)].rearrange(
                            "(ct p) d -> p ct d", p=128)),
                        waits=[("p4s", max(0, ht - 1))], inc=f"plu{sl}", inck=16)
                r.op("s", lambda e, sl=sl: e.activation(up_s8[sl][:, :], up_bf[sl][:, :], AF.Sign),
                     waits=[(f"plu{sl}", 32 * (ht // 2 + 1)),
                            ("p4t", max(0, 128 * (ht - 1)))],
                     inc="p4s")
                for i in range(NCH):
                    for ct in range(CT):
                        w = [("p4s", ht + 1), ("x1t0", 16 * (2 * i + 2)),
                             ("x1t1", 16 * (2 * i + 2))]
                        if ct == 0:
                            w.append(("p4e", max(0, 8 * ht + i - 1)))
                        r.op("t", lambda e, sl=sl, ct=ct, i=i: e.matmul(
                            ps_h[i % 2][:, :], up_s8[sl][:, 128 * ct:128 * (ct + 1)],
                            x1T_sb[:, T * ct + 512 * i:T * ct + 512 * (i + 1)],
                            start=(ct == 0), stop=(ct == CT - 1)),
                            waits=w, inc="p4t")
                    r.op("s", lambda e, sl=sl, i=i, ht=ht: e.activation(
                        hid_st[sl][:, 512 * i:512 * (i + 1)], ps_h[i % 2][:, :], AF.Relu,
                        scale=upsc_sb[:, ht:ht + 1]),
                        waits=[("p4t", r.cnt["p4t"]), ("lu", 16),
                               (f"p4sp{sl}", r.cnt[f"p4sp{sl}"] if i == 0 else 0)],
                        inc="p4e")
                r.op("sp", lambda e, ht=ht, sl=sl: e.dma_start(
                    out=hidS[ht], in_=hid_st[sl][:, :]),
                    waits=[("p4e", r.cnt["p4e"])], inc=f"p4sp{sl}", inck=16)
            r.op("sp", lambda e: e.nop(), waits=[("p4sp0", r.cnt["p4sp0"]),
                                                 ("p4sp1", r.cnt["p4sp1"]),
                                                 ("x1s0", r.cnt["x1s0"]),
                                                 ("x1s1", r.cnt["x1s1"])])
            r.emit_block()
        x1T_arena.close()

        if upto < 5:

            return nc

        with ExitStack() as p5:
            dsc_sb = p5.enter_context(nc.sbuf_tensor("dsc_sb", [1, C], F32))
            osc_sb = p5.enter_context(nc.sbuf_tensor("osc_sb", [1, 1], F32))
            osc_col = p5.enter_context(nc.sbuf_tensor("osc_col", [128, 1], F32))
            dscbc = p5.enter_context(nc.sbuf_tensor("dscbc", [128, C], F32))
            dn_bf = [p5.enter_context(nc.sbuf_tensor(f"dn_bf{i}", [128, 512], BF16)) for i in range(4)]
            dn_sq = [p5.enter_context(nc.sbuf_tensor(f"dn_sq{i}", [128, HT * 512], FP8)) for i in range(3)]
            hid_ld = [p5.enter_context(nc.sbuf_tensor(f"hid_ld{i}", [128, HT * 256], BF16)) for i in range(2)]
            x1_ld = [p5.enter_context(nc.sbuf_tensor(f"x1_ld{i}", [128, 2048], BF16)) for i in range(2)]
            tmp32 = [p5.enter_context(nc.sbuf_tensor(f"tmp32{i}", [128, 512], F32)) for i in range(2)]
            pre2_st = [p5.enter_context(nc.sbuf_tensor(f"pre2_st{i}", [128, 2048], BF16)) for i in range(2)]
            trash5 = p5.enter_context(nc.sbuf_tensor("trash5", [128, 512], BF16))
            ps_o = [p5.enter_context(nc.psum_tensor(f"ps_o{i}", [128, 512], F32)) for i in range(4)]
            r = Rec(nc, big, sems)
            r.op("sp", lambda e: e.dma_start(out=dsc_sb[:, :], in_=dsc), inc="ld5", inck=16)
            r.op("sp", lambda e: e.dma_start(out=osc_sb[:, :], in_=osc), inc="ld5", inck=16)
            r.op("t", lambda e: e.matmul(ps_o[0][:, 0:1], onescol[:, :], osc_sb[:, :],
                                         start=True, stop=True),
                 waits=[("ld5", 32)], inc="t5d")
            r.op("s", lambda e: e.copy(osc_col[:, :], ps_o[0][:, 0:1]),
                 waits=[("t5d", r.cnt["t5d"])], inc="s5d")
            for c in range(4):
                r.op("t", lambda e, c=c: e.matmul(ps_o[0][:, :], onescol[:, :],
                                                  dsc_sb[0:1, 512 * c:512 * (c + 1)],
                                                  start=True, stop=True),
                     waits=[("s5d", 1), ("d5v", r.cnt["d5v"])], inc="t5d")
                r.op("v", lambda e, c=c: e.tensor_scalar(
                    dscbc[:, 512 * c:512 * (c + 1)], ps_o[0][:, :], osc_col[:, :], None,
                    op0=ALU.mult),
                    waits=[("t5d", r.cnt["t5d"])], inc="d5v")
            nd5v = r.cnt["d5v"]

            sign_i = [0]

            def emit_sign(q, war):
                si = sign_i[0]
                dsl = si % 4
                r.op("pl", lambda e, si=si, dsl=dsl, q=q: e.dma_start(
                    out=dn_bf[dsl][:, :],
                    in_=dnT[128 * (si % HT):128 * (si % HT + 1), 512 * q:512 * (q + 1)]),
                    waits=[("p5s", max(0, si - 3))], inc=f"pld{dsl}", inck=16)
                r.op("s", lambda e, si=si, dsl=dsl, q=q: e.activation(
                    dn_sq[q % 3][:, 512 * (si % HT):512 * (si % HT + 1)],
                    dn_bf[dsl][:, :], AF.Sign),
                    waits=[(f"pld{dsl}", 16 * (si // 4 + 1)), ("p5t", war)],
                    inc="p5s")
                sign_i[0] += 1

            for ht in range(HT):
                emit_sign(0, 0)
            for ht in range(HT):
                emit_sign(1, 0)
            gidx = 0
            for hf in range(2):
                for tc in range(16):
                    hsl = tc % 2
                    wtc = max(0, r.cnt["p5t"] - 256)
                    for g in range(8):
                        eng = "sp" if g % 2 == 0 else "pl"
                        semn = f"l5s{hsl}" if g % 2 == 0 else f"l5p{hsl}"
                        r.op(eng, lambda e, tc=tc, hsl=hsl, g=g: e.dma_start(
                            out=hid_ld[hsl][:, 2048 * g:2048 * (g + 1)].rearrange(
                                "p (h t) -> p h t", t=256),
                            in_=hidS[8 * g:8 * (g + 1), :, 256 * tc:256 * (tc + 1)].rearrange(
                                "h p t -> p h t")),
                            waits=[("p5t", wtc), ("d5v", nd5v)], inc=semn, inck=16)
                    for s_ in range(2):
                        ts = 2 * tc + s_
                        r.op("sp", lambda e, ts=ts, hsl=hsl, s_=s_, hf=hf: e.dma_start(
                            out=x1_ld[hsl][:, 1024 * s_:1024 * (s_ + 1)],
                            in_=x1S[128 * ts:128 * (ts + 1), 1024 * hf:1024 * (hf + 1)]),
                            waits=[("p5v", max(0, r.cnt["p5v"] - 12))], inc=f"l5s{hsl}", inck=16)
                    nl5s = r.cnt[f"l5s{hsl}"]
                    nl5p = r.cnt[f"l5p{hsl}"]
                    wsp2 = r.cnt[f"sp2{hsl}"]
                    wst = max(0, r.cnt["p5st"] - 4)
                    for s_ in range(2):
                        ts = 2 * tc + s_
                        for c in range(2):
                            q = 2 * hf + c
                            pb = gidx % 4
                            for ht in range(HT):
                                w = [("p5s", 64 * q + ht + 1)]
                                if ht == 0:
                                    w += [(f"l5s{hsl}", nl5s), (f"l5p{hsl}", nl5p),
                                          ("p5v", max(0, 3 * (gidx - 3))),
                                          ("d5v", nd5v)]
                                r.op("t", lambda e, pb=pb, hsl=hsl, ht=ht, s_=s_, q=q:
                                     e.matmul(
                                         ps_o[pb][:, :],
                                         hid_ld[hsl][:, 256 * ht + 128 * s_:256 * ht + 128 * (s_ + 1)],
                                         dn_sq[q % 3][:, 512 * ht:512 * (ht + 1)],
                                         start=(ht == 0), stop=(ht == HT - 1)),
                                     waits=w, inc="p5t")
                            cg = 2 * hf + c
                            r.op("v", lambda e, pb=pb, cg=cg: e.tensor_tensor(
                                tmp32[pb % 2][:, :], ps_o[pb][:, :],
                                dscbc[:, 512 * cg:512 * (cg + 1)], op=ALU.mult),
                                waits=[("p5t", r.cnt["p5t"])], inc="p5v")
                            r.op("v", lambda e, pb=pb, hsl=hsl, s_=s_, c=c: e.tensor_tensor(
                                pre2_st[hsl][:, 1024 * s_ + 512 * c:1024 * s_ + 512 * (c + 1)],
                                tmp32[pb % 2][:, :],
                                x1_ld[hsl][:, 1024 * s_ + 512 * c:1024 * s_ + 512 * (c + 1)],
                                op=ALU.add),
                                waits=[("p5v", r.cnt["p5v"]),
                                       (f"sp2{hsl}", wsp2), ("p5st", wst)], inc="p5v")
                            col = 4 * ts + 2 * hf + c
                            r.op("v", lambda e, hsl=hsl, s_=s_, c=c, col=col: e.reduce_sum(
                                m2p[:, col:col + 1],
                                pre2_st[hsl][:, 1024 * s_ + 512 * c:1024 * s_ + 512 * (c + 1)],
                                axis=mybir.AxisListType.X),
                                waits=[("p5v", r.cnt["p5v"])], inc="p5v")
                            r.op("s", lambda e, hsl=hsl, s_=s_, c=c, col=col: e.activation(
                                trash5[:, :],
                                pre2_st[hsl][:, 1024 * s_ + 512 * c:1024 * s_ + 512 * (c + 1)],
                                AF.Square, accum_out=s2p[:, col:col + 1]),
                                waits=[("p5v", r.cnt["p5v"]), ("p5st", r.cnt["p5st"])],
                                inc="p5st")
                            gidx += 1
                        r.op("s", lambda e, ts=ts, hsl=hsl, s_=s_, hf=hf: e.dma_start(
                            out=pre2S[128 * ts:128 * (ts + 1), 1024 * hf:1024 * (hf + 1)],
                            in_=pre2_st[hsl][:, 1024 * s_:1024 * (s_ + 1)]),
                            waits=[("p5v", r.cnt["p5v"]), ("p5st", r.cnt["p5st"])],
                            inc=f"sp2{hsl}", inck=16)
                    if hf == 0 and tc % 4 == 3:
                        for _ in range(16):
                            emit_sign(2, 0)
                if hf == 0:
                    war3 = r.cnt["p5t"]
                    for ht in range(HT):
                        emit_sign(3, war3)
            r.op("sp", lambda e: e.nop(), waits=[("sp20", r.cnt["sp20"]),
                                                 ("sp21", r.cnt["sp21"])])
            r.emit_block()

        if upto < 6:
            return nc

        with ExitStack() as p6:
            pre2_ld = [p6.enter_context(nc.sbuf_tensor(f"pre2_ld{i}", [128, 2048], BF16)) for i in range(2)]
            out_st = [p6.enter_context(nc.sbuf_tensor(f"out_st{i}", [128, 2048], F32)) for i in range(2)]
            r = Rec(nc, big, sems)
            for ts in range(NTS):
                r.op("v", lambda e, ts=ts: e.reduce_sum(
                    m2a[:, ts:ts + 1], m2p[:, 4 * ts:4 * (ts + 1)], axis=mybir.AxisListType.X),
                    inc="b6")
                r.op("v", lambda e, ts=ts: e.reduce_sum(
                    v2a[:, ts:ts + 1], s2p[:, 4 * ts:4 * (ts + 1)], axis=mybir.AxisListType.X),
                    inc="b6")
            n6 = r.cnt["b6"]
            r.op("v", lambda e: e.tensor_scalar(m2a[:, :], m2a[:, :], 1.0 / C, None, op0=ALU.mult),
                 waits=[("b6", n6)], inc="b6")
            r.op("v", lambda e: e.tensor_scalar(v2a[:, :], v2a[:, :], 1.0 / C, None, op0=ALU.mult),
                 waits=[("b6", r.cnt["b6"])], inc="b6")
            r.op("v", lambda e: e.tensor_tensor(r2a[:, :], m2a[:, :], m2a[:, :], op=ALU.mult),
                 waits=[("b6", r.cnt["b6"])], inc="b6")
            r.op("v", lambda e: e.tensor_tensor(v2a[:, :], v2a[:, :], r2a[:, :], op=ALU.subtract),
                 waits=[("b6", r.cnt["b6"])], inc="b6")
            r.op("s", lambda e: e.activation(r2a[:, :], v2a[:, :], AF.Ln, bias=cbias[:, 3:4]),
                 waits=[("b6", r.cnt["b6"])], inc="s6")
            r.op("s", lambda e: e.activation(r2a[:, :], r2a[:, :], AF.Exp, scale=-0.5),
                 waits=[("s6", r.cnt["s6"])], inc="s6")
            ns6 = r.cnt["s6"]
            for ts in range(NTS):
                sl = ts % 2
                wo6 = r.cnt[f"o6{sl}"]
                r.op("pl", lambda e, ts=ts, sl=sl: e.dma_start(
                    out=pre2_ld[sl][:, :], in_=pre2S[128 * ts:128 * (ts + 1), :]),
                    waits=[("v6", max(0, 4 * (ts - 1)))], inc=f"l6{sl}", inck=16)
                nl6 = r.cnt[f"l6{sl}"]
                for c in range(4):
                    r.op("v", lambda e, sl=sl, ts=ts, c=c: e.tensor_scalar(
                        out_st[sl][:, 512 * c:512 * (c + 1)],
                        pre2_ld[sl][:, 512 * c:512 * (c + 1)],
                        m2a[:, ts:ts + 1], r2a[:, ts:ts + 1],
                        op0=ALU.subtract, op1=ALU.mult),
                        waits=[(f"l6{sl}", nl6), ("s6", ns6),
                               (f"o6{sl}", wo6)], inc="v6")
                r.op("sp", lambda e, ts=ts, sl=sl: e.dma_start(
                    out=out[128 * ts:128 * (ts + 1), :], in_=out_st[sl][:, :]),
                    waits=[("v6", r.cnt["v6"])], inc=f"o6{sl}", inck=16)
            r.op("sp", lambda e: e.nop(), waits=[("o60", r.cnt["o60"]),
                                                 ("o61", r.cnt["o61"])])
            r.emit_block()

    return nc


def make_inputs(hs_b, Wq, Wk, Wv, Wg, Wc, bc, upw, dnw, ups, dns, oscl):
    f = np.float32
    sel = np.zeros((8, 1024), dtype=f)
    for h in range(8):
        sel[h, 128 * h:128 * (h + 1)] = 1.0
    return {
        "x": np.ascontiguousarray(hs_b, dtype=f),
        "xT": np.ascontiguousarray(np.asarray(hs_b, dtype=f).T),
        "Wq": np.ascontiguousarray(Wq, dtype=f),
        "Wk": np.ascontiguousarray(Wk, dtype=f),
        "Wv": np.ascontiguousarray(Wv, dtype=f),
        "WgR": np.ascontiguousarray(
            np.asarray(Wg, dtype=f).reshape(16, 128, 8).transpose(1, 0, 2).reshape(128, 128)),
        "Wc": np.ascontiguousarray(Wc, dtype=f),
        "bc": np.ascontiguousarray(np.asarray(bc, dtype=f).reshape(1, C)),
        "upT": np.ascontiguousarray(np.asarray(upw, dtype=f).T),
        "dnT": np.ascontiguousarray(np.asarray(dnw, dtype=f).T),
        "upsc": np.ascontiguousarray(np.asarray(ups, dtype=f).reshape(64, 128).T),
        "dsc": np.ascontiguousarray(np.asarray(dns, dtype=f).reshape(1, C)),
        "osc": np.ascontiguousarray(np.asarray(oscl, dtype=f).reshape(1, 1)),
        "sel": sel,
        "tri": np.triu(np.ones((128, 128), dtype=f)),
        "ident": np.eye(128, dtype=f),
        "onesc": np.ones((1, 128), dtype=f),
    }


def kernel(L0_attn_hidden_states, L0_attn_Wq, L0_attn_Wk, L0_attn_Wv, L0_attn_Wg,
           L0_attn_Wc, L0_attn_bc, L0_ffn_up_weight, L0_ffn_down_weight,
           L0_ffn_up_scale, L0_ffn_down_scale, L0_ffn_output_scale):
    hs = np.asarray(L0_attn_hidden_states, dtype=np.float32)
    n = 8
    in_maps = [make_inputs(hs[b], L0_attn_Wq, L0_attn_Wk, L0_attn_Wv, L0_attn_Wg,
                           L0_attn_Wc, L0_attn_bc, L0_ffn_up_weight, L0_ffn_down_weight,
                           L0_ffn_up_scale, L0_ffn_down_scale, L0_ffn_output_scale)
               for b in range(n)]
    nc = build(upto=6)
    res = run_bass_kernel_spmd(nc, in_maps, core_ids=list(range(n)))
    return np.stack([np.asarray(res.results[b]["out"], dtype=np.float32)
                     for b in range(n)], axis=0)


# version 11
# speedup vs baseline: 1.0030x; 1.0030x over previous
from collections import defaultdict
from contextlib import ExitStack

import numpy as np

import concourse.bass as bass
import concourse.mybir as mybir
from concourse.bass_utils import run_bass_kernel_spmd

F32 = mybir.dt.float32
BF16 = mybir.dt.bfloat16
FP8 = mybir.dt.float8e4
AF = mybir.ActivationFunctionType
ALU = mybir.AluOpType

T, C, H, HR, D, DH = 4096, 2048, 8, 16, 128, 8192
NCH, NTS, CT, HT = 8, 32, 16, 64
OFF = 6.906768
LNS = float(np.log(1.0 / np.sqrt(D)))
EPS_LN = 1e-5


class Rec:

    ENGS = ("t", "v", "s", "sp", "pl")

    def __init__(self, nc, semstack, sems):
        self.nc = nc
        self.ops = {e: [] for e in self.ENGS}
        self.sems = sems
        self.semstack = semstack
        self.cnt = defaultdict(int)

    def sem(self, name):
        if name not in self.sems:
            self.sems[name] = self.semstack.enter_context(self.nc.semaphore(name))
        return self.sems[name]

    def op(self, eng, fn, waits=(), inc=None, inck=1):
        w = [(n, v) for (n, v) in waits if v > 0]
        for n, _ in w:
            self.sem(n)
        if inc is not None:
            self.sem(inc)
            self.cnt[inc] += inck
        self.ops[eng].append((fn, w, inc, inck))
        return self.cnt[inc] if inc is not None else None

    def replay(self, eng_name, eng):
        done = defaultdict(int)
        for fn, waits, inc, inck in self.ops[eng_name]:
            for n, v in waits:
                if v > done[n]:
                    eng.wait_ge(self.sems[n], v)
                    done[n] = v
            instr = fn(eng)
            if inc is not None:
                instr.then_inc(self.sems[inc], inck)

    def emit_block(self):
        with self.nc.Block() as blk:
            @blk.tensor
            def _(t):
                self.replay("t", t)

            @blk.vector
            def _(v):
                self.replay("v", v)

            @blk.scalar
            def _(s):
                self.replay("s", s)

            @blk.sync
            def _(sp):
                self.replay("sp", sp)

            @blk.gpsimd
            def _(pl):
                self.replay("pl", pl)


def build(upto=6, dbg_out=()):
    nc = bass.Bass("TRN2", target_bir_lowering=False)
    dbg_out = set(dbg_out)

    def dram(name, shape, dtype, kind="Internal"):
        k = "ExternalOutput" if name in dbg_out else kind
        return nc.dram_tensor(name, shape, dtype, kind=k).ap()

    xT = dram("xT", [C, T], F32, "ExternalInput")
    x_in = dram("x", [T, C], F32, "ExternalInput")
    Wq = dram("Wq", [C, 2048], F32, "ExternalInput")
    Wk = dram("Wk", [C, 1024], F32, "ExternalInput")
    Wv = dram("Wv", [C, 1024], F32, "ExternalInput")
    WgR = dram("WgR", [128, 128], F32, "ExternalInput")
    Wc = dram("Wc", [2048, C], F32, "ExternalInput")
    bc = dram("bc", [1, C], F32, "ExternalInput")
    upT = dram("upT", [C, DH], F32, "ExternalInput")
    dnT = dram("dnT", [DH, C], F32, "ExternalInput")
    upsc = dram("upsc", [128, 64], F32, "ExternalInput")
    dsc = dram("dsc", [1, C], F32, "ExternalInput")
    osc = dram("osc", [1, 1], F32, "ExternalInput")
    sel = dram("sel", [8, 1024], F32, "ExternalInput")
    tri = dram("tri", [128, 128], F32, "ExternalInput")
    ident = dram("ident", [128, 128], F32, "ExternalInput")
    onesc = dram("onesc", [1, 128], F32, "ExternalInput")
    out = dram("out", [T, C], F32, "ExternalOutput")

    qS = dram("qS", [HR, 128, T], BF16)
    kS = dram("kS", [H, 128, T], BF16)
    vS = dram("vS", [H, 128, 33 * 128], BF16)
    yTS = dram("yTS", [HR, 128, T], BF16)
    pre1S = dram("pre1S", [T, C], BF16)
    x1S = dram("x1S", [T, C], BF16)
    hidS = dram("hidS", [HT, 128, T], BF16)
    pre2S = dram("pre2S", [T, C], BF16)

    big = ExitStack()
    sems = {}
    with big:
        onescol = big.enter_context(nc.sbuf_tensor("onescol", [1, 128], F32))
        tri_bf = big.enter_context(nc.sbuf_tensor("tri_bf", [128, 128], BF16))
        sel_bf = big.enter_context(nc.sbuf_tensor("sel_bf", [8, 1024], BF16))
        id_sb = big.enter_context(nc.sbuf_tensor("id_sb", [128, 8], F32))
        cbias = big.enter_context(nc.sbuf_tensor("cbias", [128, 4], F32))
        mp1 = big.enter_context(nc.sbuf_tensor("mp1", [128, 128], F32))
        sq1 = big.enter_context(nc.sbuf_tensor("sq1", [128, 128], F32))
        m1a = big.enter_context(nc.sbuf_tensor("m1a", [128, 32], F32))
        v1a = big.enter_context(nc.sbuf_tensor("v1a", [128, 32], F32))
        r1a = big.enter_context(nc.sbuf_tensor("r1a", [128, 32], F32))
        m2p = big.enter_context(nc.sbuf_tensor("m2p", [128, 128], F32))
        s2p = big.enter_context(nc.sbuf_tensor("s2p", [128, 128], F32))
        m2a = big.enter_context(nc.sbuf_tensor("m2a", [128, 32], F32))
        v2a = big.enter_context(nc.sbuf_tensor("v2a", [128, 32], F32))
        r2a = big.enter_context(nc.sbuf_tensor("r2a", [128, 32], F32))

        with ExitStack() as a_arena:
            xT_sb = a_arena.enter_context(nc.sbuf_tensor("xT_sb", [128, CT * T], BF16))
            expq = a_arena.enter_context(nc.sbuf_tensor("expq", [8, T], BF16))
            col_e = a_arena.enter_context(nc.sbuf_tensor("col_e", [128, 8 * NTS], F32))
            with ExitStack() as p1:
                wg_bf = p1.enter_context(nc.sbuf_tensor("wg_bf", [128, 128], BF16))
                gstk = p1.enter_context(nc.sbuf_tensor("gstk", [128, 2 * T], F32))
                gP = gstk[0:8, 0:T]
                gQ = gstk[0:8, T:2 * T]
                gR = gstk[32:40, 0:T]
                gS = gstk[32:40, T:2 * T]
                bcH = p1.enter_context(nc.sbuf_tensor("bcH", [128, T], BF16))
                w_bf = [p1.enter_context(nc.sbuf_tensor(f"w_bf{i}", [128, 2048], BF16)) for i in range(2)]
                q_st = [p1.enter_context(nc.sbuf_tensor(f"q_st{i}", [128, T], BF16)) for i in range(2)]
                ps_g = p1.enter_context(nc.psum_tensor("ps_g", [8, 512], F32))
                ps_ct = [p1.enter_context(nc.psum_tensor(f"ps_ct{i}", [128, 8], F32)) for i in range(2)]
                ps_q = [p1.enter_context(nc.psum_tensor(f"ps_q{i}", [128, 512], F32)) for i in range(2)]
                ps_bc = [p1.enter_context(nc.psum_tensor(f"ps_bc{i}", [128, 512], F32)) for i in range(2)]

                r = Rec(nc, big, sems)
                for ap_in, ap_out in ((onesc, onescol[:, :]),
                                      (ident[0:8, 0:8], id_sb[0:8, :])):
                    r.op("sp", lambda e, a=ap_in, o=ap_out: e.dma_start(out=o, in_=a),
                         inc="ldc", inck=16)
                nldc = r.cnt["ldc"]
                for ci, val in ((0, OFF), (1, -OFF), (2, LNS), (3, EPS_LN)):
                    r.op("v", lambda e, ci=ci, val=val: e.memset(cbias[:, ci:ci + 1], val),
                         inc="cb")
                ncb = r.cnt["cb"]
                r.op("pl", lambda e: e.dma_start(out=wg_bf[:, :], in_=WgR), inc="plk", inck=16)
                r.op("pl", lambda e: e.dma_start(out=sel_bf[:, :], in_=sel), inc="plk", inck=16)
                r.op("pl", lambda e: e.dma_start(out=tri_bf[:, :], in_=tri), inc="plk", inck=16)
                for ct in range(CT):
                    r.op("pl", lambda e, ct=ct: e.dma_start(
                        out=xT_sb[:, T * ct:T * (ct + 1)], in_=xT[128 * ct:128 * (ct + 1), :]),
                        inc="plx", inck=16)
                nplx_all = r.cnt["plx"]

                for i in range(NCH):
                    for ct in range(CT):
                        r.op("t", lambda e, i=i, ct=ct: e.matmul(
                            ps_g[:, :], wg_bf[:, 8 * ct:8 * ct + 8],
                            xT_sb[:, T * ct + 512 * i: T * ct + 512 * (i + 1)],
                            start=(ct == 0), stop=(ct == CT - 1)),
                            waits=[("plx", nplx_all), ("plk", 48), ("ldc", nldc),
                                   ("sg", r.cnt["sg"])],
                            inc="tg" if ct == CT - 1 else None)
                    r.op("s", lambda e, i=i: e.copy(gP[:, 512 * i:512 * (i + 1)], ps_g[:, :]),
                         waits=[("tg", r.cnt["tg"])], inc="sg")
                nsg = r.cnt["sg"]
                r.op("s", lambda e: e.activation(gQ, gP, AF.Abs, bias=cbias[0:8, 0:1]),
                     waits=[("sg", nsg), ("cb", ncb)], inc="sg")
                r.op("s", lambda e: e.activation(gR, gQ, AF.Exp, scale=-1.0),
                     waits=[("sg", r.cnt["sg"])], inc="sg")
                r.op("s", lambda e: e.activation(gS, gR, AF.Ln, bias=1.0),
                     waits=[("sg", r.cnt["sg"])], inc="sg")
                r.op("s", lambda e: e.activation(gR, gP, AF.Relu, scale=-1.0,
                                                 bias=cbias[0:8, 1:2]),
                     waits=[("sg", r.cnt["sg"])], inc="sg")
                r.op("v", lambda e: e.tensor_tensor_scan(gQ, gS, gR, 0.0,
                                                         op0=ALU.add, op1=ALU.add),
                     waits=[("sg", r.cnt["sg"])], inc="vm")
                nscan = r.cnt["vm"]
                r.op("s", lambda e: e.activation(expq[:, :], gQ, AF.Exp, scale=-0.5,
                                                 bias=cbias[0:8, 2:3]),
                     waits=[("vm", nscan)], inc="sg")
                nexpq = r.cnt["sg"]
                for ts in range(NTS):
                    r.op("t", lambda e, ts=ts: e.transpose(
                        ps_ct[ts % 2][:, :], gstk[0:8, T + 128 * ts:T + 128 * (ts + 1)],
                        id_sb[0:8, :]),
                        waits=[("vm", nscan), ("sce", max(0, r.cnt["sce"] - 1))], inc="tct")
                    r.op("s", lambda e, ts=ts: e.activation(
                        col_e[:, 8 * ts:8 * (ts + 1)], ps_ct[ts % 2][:, :], AF.Exp),
                        waits=[("tct", r.cnt["tct"])], inc="sce")

                for u in range(HR + H):
                    is_q = u < HR
                    w_src = Wq if is_q else Wk
                    col0 = 128 * u if is_q else 128 * (u - HR)
                    buf = u % 2
                    for half in range(2):
                        r.op("pl", lambda e, w=w_src, c0=col0, hh=half, b=buf: e.dma_start(
                            out=w_bf[b][:, 1024 * hh:1024 * (hh + 1)].rearrange(
                                "p (ct d) -> p ct d", d=128),
                            in_=w[1024 * hh:1024 * (hh + 1), c0:c0 + 128].rearrange(
                                "(ct p) d -> p ct d", p=128)),
                            waits=[("tp", max(0, 8 * (u - 1)))],
                            inc=f"plw{buf}", inck=16)
                    if is_q and u % 2 == 0:
                        head = u // 2
                        for i in range(NCH):
                            r.op("t", lambda e, hd=head, i=i: e.matmul(
                                ps_bc[i % 2][:, :], sel_bf[0:8, 128 * hd:128 * (hd + 1)],
                                expq[0:8, 512 * i:512 * (i + 1)], start=True, stop=True),
                                waits=[("sg", nexpq), ("plk", 48),
                                       ("sbc", max(0, r.cnt["sbc"] - 1)),
                                       ("vq", 8 * u)], inc="tbc")
                            r.op("s", lambda e, i=i: e.copy(bcH[:, 512 * i:512 * (i + 1)],
                                                            ps_bc[i % 2][:, :]),
                                 waits=[("tbc", r.cnt["tbc"])], inc="sbc")
                    for i in range(NCH):
                        gq = NCH * u + i
                        for ct in range(CT):
                            r.op("t", lambda e, b=buf, ct=ct, i=i: e.matmul(
                                ps_q[i % 2][:, :], w_bf[b][:, 128 * ct:128 * (ct + 1)],
                                xT_sb[:, T * ct + 512 * i:T * ct + 512 * (i + 1)],
                                start=(ct == 0), stop=(ct == CT - 1)),
                                waits=[(f"plw{u % 2}", 32 * (u // 2 + 1)),
                                       ("vq", max(0, gq - 1))],
                                inc="tp" if ct == CT - 1 else None)
                        if is_q:
                            r.op("v", lambda e, b=buf, i=i: e.tensor_tensor(
                                q_st[b][:, 512 * i:512 * (i + 1)], ps_q[i % 2][:, :],
                                bcH[:, 512 * i:512 * (i + 1)], op=ALU.mult),
                                waits=[("tp", r.cnt["tp"]), ("sbc", 8 * (u // 2 + 1)),
                                       (f"spq{u % 2}", max(0, 16 * (u // 2)))], inc="vq")
                        else:
                            r.op("v", lambda e, b=buf, i=i: e.tensor_copy(
                                q_st[b][:, 512 * i:512 * (i + 1)], ps_q[i % 2][:, :]),
                                waits=[("tp", r.cnt["tp"]),
                                       (f"spq{u % 2}", max(0, 16 * (u // 2)))],
                                inc="vq")
                    dst = qS[u] if is_q else kS[u - HR]
                    r.op("sp", lambda e, b=buf, d=dst: e.dma_start(out=d, in_=q_st[b][:, :]),
                         waits=[("vq", r.cnt["vq"])], inc=f"spq{buf}", inck=16)
                r.op("sp", lambda e: e.nop(), waits=[("spq0", r.cnt["spq0"]),
                                                     ("spq1", r.cnt["spq1"])])
                r.emit_block()

            with ExitStack() as p1b:
                wv_bf = p1b.enter_context(nc.sbuf_tensor("wv_bf", [128, CT * 512], BF16))
                vv_st = p1b.enter_context(nc.sbuf_tensor("vv_st", [128, 4 * 33 * 128], BF16))
                ps_v = [p1b.enter_context(nc.psum_tensor(f"ps_v{i}", [128, 512], F32)) for i in range(2)]
                r = Rec(nc, big, sems)
                VW = 33 * 128
                for hg in range(2):
                    for half in range(2):
                        r.op("pl", lambda e, hg=hg, hh=half: e.dma_start(
                            out=wv_bf[:, CT * 256 * hh:CT * 256 * (hh + 1)].rearrange(
                                "p (ct d) -> p ct d", d=512),
                            in_=Wv[1024 * hh:1024 * (hh + 1), 512 * hg:512 * (hg + 1)].rearrange(
                                "(ct p) d -> p ct d", p=128)),
                            waits=[("tv", max(0, 32 * hg))], inc="plv", inck=16)
                    r.op("v", lambda e: e.memset(vv_st[:, :], 0.0),
                         waits=[("spv", 64 * hg)], inc="vv")
                    n_ms = r.cnt["vv"]
                    for ts in range(NTS):
                        for ct in range(CT):
                            r.op("t", lambda e, ts=ts, ct=ct: e.matmul(
                                ps_v[ts % 2][:, :],
                                xT_sb[:, T * ct + 128 * ts:T * ct + 128 * (ts + 1)],
                                wv_bf[:, 512 * ct:512 * (ct + 1)],
                                start=(ct == 0), stop=(ct == CT - 1)),
                                waits=[("plv", 32 * (hg + 1)),
                                       ("vv", max(0, r.cnt["vv"] - 8))],
                                inc="tv" if ct == CT - 1 else None)
                        for hl in range(4):
                            head = 4 * hg + hl
                            r.op("v", lambda e, ts=ts, hl=hl, hd=head: e.tensor_scalar(
                                vv_st[:, VW * hl + 132 * ts:VW * hl + 132 * ts + 128],
                                ps_v[ts % 2][:, 128 * hl:128 * (hl + 1)],
                                col_e[:, 8 * ts + hd:8 * ts + hd + 1], None, op0=ALU.mult),
                                waits=[("tv", r.cnt["tv"]), ("vv", n_ms)], inc="vv")
                            r.op("v", lambda e, ts=ts, hl=hl, hd=head: e.tensor_copy(
                                vv_st[:, VW * hl + 132 * ts + 128:VW * hl + 132 * ts + 129],
                                col_e[:, 8 * ts + hd:8 * ts + hd + 1]),
                                inc="vv")
                    for hl in range(4):
                        r.op("sp", lambda e, hg=hg, hl=hl: e.dma_start(
                            out=vS[4 * hg + hl], in_=vv_st[:, VW * hl:VW * (hl + 1)]),
                            waits=[("vv", r.cnt["vv"])], inc="spv", inck=16)
                r.op("sp", lambda e: e.nop(), waits=[("spv", r.cnt["spv"])])
                r.emit_block()

        if upto < 2:
            return nc

        with ExitStack() as p2:
            qA = [p2.enter_context(nc.sbuf_tensor(f"qA{i}", [128, T], BF16)) for i in range(2)]
            qB = [p2.enter_context(nc.sbuf_tensor(f"qB{i}", [128, T], BF16)) for i in range(2)]
            kk = [p2.enter_context(nc.sbuf_tensor(f"kk{i}", [128, T], BF16)) for i in range(2)]
            vv = [p2.enter_context(nc.sbuf_tensor(f"vv{i}", [128, 33 * 128], BF16)) for i in range(2)]
            A_buf = [p2.enter_context(nc.sbuf_tensor(f"A_buf{i}", [128, 32 * 512], BF16)) for i in range(2)]
            y_sb = [p2.enter_context(nc.sbuf_tensor(f"y_sb{i}", [128, T], BF16)) for i in range(2)]
            yT_sb = [p2.enter_context(nc.sbuf_tensor(f"yT_sb{i}", [128, T], BF16)) for i in range(2)]
            vsq = p2.enter_context(nc.sbuf_tensor("vsq", [128, 1024], F32))
            ncol = p2.enter_context(nc.sbuf_tensor("ncol", [128, 1], F32))
            rcol = p2.enter_context(nc.sbuf_tensor("rcol", [128, 1], F32))
            psS = [p2.enter_context(nc.psum_tensor(f"psS{i}", [128, 1024], F32)) for i in range(3)]
            psY = [p2.enter_context(nc.psum_tensor(f"psY{i}", [128, 129], F32)) for i in range(2)]
            r = Rec(nc, big, sems)

            NPS = 3
            gp = [0]
            pair_free = {}
            G = [0]

            def emit_S_and_squares(h, sig, i, slot):
                q_sb = qA[slot] if sig == 0 else qB[slot]
                nj = 4 * i + 4
                npair = 2 * i
                avbase = r.cnt["avT"]
                pair_eng = ["v" if (p % 3 == 2) else "s" for p in range(npair)]
                tile_done = {}
                cnt_s = r.cnt["sq"]
                cnt_vm = r.cnt["sqv"]
                cnt_vc = r.cnt["sqc"]
                my_gp = {}
                for p in range(npair + 2):
                    my_gp[p] = gp[0]
                    gp[0] += 1
                for p in range(npair):
                    if pair_eng[p] == "v":
                        cnt_vc += 1
                        cnt_vm += 1
                        pair_free[my_gp[p]] = ("sqc", cnt_vc)
                        tile_done[2 * p] = ("sqv", cnt_vm)
                        tile_done[2 * p + 1] = ("sqv", cnt_vm)
                    else:
                        cnt_s += 1
                        pair_free[my_gp[p]] = ("sq", cnt_s)
                        tile_done[2 * p] = ("sq", cnt_s)
                        tile_done[2 * p + 1] = ("sq", cnt_s)
                for rr in range(4):
                    cnt_s += 1
                    j = 4 * i + rr
                    pair_free[my_gp[j // 2]] = ("sq", cnt_s)
                    tile_done[j] = ("sq", cnt_s)
                sbase = r.cnt["sS"]
                for j in range(nj):
                    p = j // 2
                    a = my_gp[p] % NPS
                    cslot = j % 2
                    rr = j - 4 * i
                    c0 = 128 * rr if rr >= 0 else 0
                    w = []
                    if my_gp[p] - NPS in pair_free:
                        w.append(pair_free[my_gp[p] - NPS])
                    if j == 0 and i == 0 and sig == 0:
                        w.append((f"ld2{slot}", 64 * (h // 2 + 1)))
                    do_inc = (rr >= 0) or (j % 2 == 1)
                    r.op("t", lambda e, a=a, cs=cslot, j=j, c0=c0, i=i, q=q_sb,
                         ks=kk[slot]: e.matmul(
                             psS[a][:, 512 * cs + c0:512 * (cs + 1)],
                             ks[:, 128 * j:128 * (j + 1)],
                             q[:, 512 * i + c0:512 * (i + 1)],
                             start=True, stop=True),
                         waits=w, inc="sS" if do_inc else None)
                for p in range(npair):
                    a = my_gp[p] % NPS
                    if pair_eng[p] == "v":
                        r.op("v", lambda e, a=a: e.tensor_copy(
                            vsq[:, :], psS[a][:, :]),
                            waits=[("sS", sbase + p + 1), ("avT", avbase),
                                   ("sqv", r.cnt["sqv"])],
                            inc="sqc")
                        r.op("v", lambda e, p=p, sg=sig: e.tensor_tensor(
                            A_buf[sg][:, 1024 * p:1024 * (p + 1)], vsq[:, :],
                            vsq[:, :], op=ALU.mult),
                            waits=[("sqc", r.cnt["sqc"]), ("avT", avbase)],
                            inc="sqv")
                    else:
                        r.op("s", lambda e, a=a, p=p, sg=sig: e.activation(
                            A_buf[sg][:, 1024 * p:1024 * (p + 1)], psS[a][:, :],
                            AF.Square),
                            waits=[("sS", sbase + p + 1), ("avT", avbase)],
                            inc="sq")
                for rr in range(4):
                    j = 4 * i + rr
                    a = my_gp[j // 2] % NPS
                    cslot = j % 2
                    c0 = 128 * rr
                    r.op("s", lambda e, a=a, cs=cslot, j=j, c0=c0, sg=sig: e.activation(
                        A_buf[sg][:, 512 * j + c0:512 * (j + 1)],
                        psS[a][:, 512 * cs + c0:512 * (cs + 1)], AF.Square),
                        waits=[("sS", sbase + npair + (j - 4 * i) + 1), ("avT", avbase)],
                        inc="sq")
                for rr in range(4):
                    j = 4 * i + rr
                    r.op("v", lambda e, j=j, rr=rr, sg=sig: e.tensor_tensor(
                        A_buf[sg][:, 512 * j + 128 * rr:512 * j + 128 * (rr + 1)],
                        A_buf[sg][:, 512 * j + 128 * rr:512 * j + 128 * (rr + 1)],
                        tri_bf[:, :], op=ALU.mult),
                        waits=[tile_done[j]], inc="msk")
                mskbase = r.cnt["msk"] - 4
                return tile_done, mskbase

            def emit_AV(h, sig, i, slot, u, tile_done, mskbase):
                for sub in range(4):
                    Gi = G[0]
                    G[0] += 1
                    b = Gi % 2
                    njs = 4 * i + sub + 1
                    for j in range(njs):
                        w = [tile_done[j], ("ye", max(0, 2 * (Gi - 1)))]
                        if j == 4 * i + sub:
                            w.append(("msk", mskbase + sub + 1))
                        r.op("t", lambda e, b=b, j=j, sub=sub, vs=vv[slot], njs=njs,
                             sg=sig: e.matmul(
                                 psY[b][:, :],
                                 A_buf[sg][:, 512 * j + 128 * sub:512 * j + 128 * (sub + 1)],
                                 vs[:, 132 * j:132 * j + 129],
                                 start=(j == 0), stop=(j == njs - 1)),
                             waits=w, inc="avT" if j == njs - 1 else None)
                    ts_g = 4 * i + sub
                    r.op("s", lambda e, b=b: e.activation(
                        ncol[:, :], psY[b][:, 128:129], AF.Copy, bias=1e-6),
                        waits=[("avT", r.cnt["avT"]), ("ye", max(0, r.cnt["ye"] - 1))],
                        inc="yes")
                    r.op("v", lambda e: e.reciprocal(rcol[:, :], ncol[:, :]),
                         waits=[("yes", r.cnt["yes"]), ("ye", r.cnt["ye"])], inc="ye")
                    r.op("v", lambda e, b=b, sg=sig, ts_g=ts_g: e.tensor_scalar(
                        y_sb[sg][:, 128 * ts_g:128 * (ts_g + 1)], psY[b][:, 0:128],
                        rcol[:, :], None, op0=ALU.mult),
                        waits=[("ye", r.cnt["ye"]),
                               (f"ytp{sig}", r.cnt[f"ytp{sig}"])], inc="ye")

            for h in range(H):
                wait_av = 64 * (h - 1) if h >= 2 else 0
                slot = h % 2
                for src_, dst in ((qS[2 * h], qA[slot][:, :]), (qS[2 * h + 1], qB[slot][:, :]),
                                  (kS[h], kk[slot][:, :]), (vS[h], vv[slot][:, :])):
                    r.op("pl", lambda e, s=src_, d=dst: e.dma_start(out=d, in_=s),
                         waits=[("avT", wait_av)], inc=f"ld2{slot}", inck=16)
                for i in range(NCH):
                    plans = {}
                    for sig in range(2):
                        plans[sig] = emit_S_and_squares(h, sig, i, slot)
                    for sig in range(2):
                        td, mb = plans[sig]
                        emit_AV(h, sig, i, slot, 2 * h + sig, td, mb)
                for sig in range(2):
                    u = 2 * h + sig
                    ye_all = r.cnt["ye"]
                    ntp = r.op("sp", lambda e, sg=sig: e.dma_start(
                        out=yT_sb[sg][:, :].rearrange("p (n f) -> p n f", f=128),
                        in_=y_sb[sg][:, :].rearrange("p (n f) -> p n f", f=128),
                        transpose=True),
                        waits=[("ye", ye_all), (f"ysp{sig}", r.cnt[f"ysp{sig}"])],
                        inc=f"ytp{sig}", inck=16)
                    r.op("sp", lambda e, u=u, sg=sig: e.dma_start(
                        out=yTS[u], in_=yT_sb[sg][:, :]),
                        waits=[(f"ytp{sig}", ntp)], inc=f"ysp{sig}", inck=16)
            r.op("sp", lambda e: e.nop(), waits=[("ysp0", r.cnt["ysp0"]),
                                                 ("ysp1", r.cnt["ysp1"])])
            r.emit_block()

        if upto < 3:
            return nc

        with ExitStack() as p3:
            bc_sb = p3.enter_context(nc.sbuf_tensor("bc_sb", [1, C], F32))
            bcbc = p3.enter_context(nc.sbuf_tensor("bcbc", [128, C], F32))
            wc_bf = p3.enter_context(nc.sbuf_tensor("wc_bf", [128, 16 * 2048], BF16))
            yT_ld = [p3.enter_context(nc.sbuf_tensor(f"yT_ld{i}", [128, 2048], BF16)) for i in range(3)]
            x_ld = [p3.enter_context(nc.sbuf_tensor(f"x_ld{i}", [128, 2048], F32)) for i in range(3)]
            t1_sb = [p3.enter_context(nc.sbuf_tensor(f"t1_sb{i}", [128, 512], F32)) for i in range(2)]
            pre1_st = [p3.enter_context(nc.sbuf_tensor(f"pre1_st{i}", [128, 2048], BF16)) for i in range(2)]
            trash = p3.enter_context(nc.sbuf_tensor("trash", [128, 512], BF16))
            ps_att = [p3.enter_context(nc.psum_tensor(f"ps_att{i}", [128, 512], F32)) for i in range(2)]
            r = Rec(nc, big, sems)
            for s in range(16):
                r.op("pl", lambda e, s=s: e.dma_start(
                    out=wc_bf[:, 2048 * s:2048 * (s + 1)], in_=Wc[128 * s:128 * (s + 1), :]),
                    inc="plc", inck=16)
            r.op("sp", lambda e: e.dma_start(out=bc_sb[:, :], in_=bc), inc="lb", inck=16)
            for c in range(4):
                r.op("t", lambda e, c=c: e.matmul(
                    ps_att[0][:, :], onescol[:, :], bc_sb[0:1, 512 * c:512 * (c + 1)],
                    start=True, stop=True),
                    waits=[("lb", 16), ("s3b", r.cnt["s3b"])], inc="t3b")
                r.op("s", lambda e, c=c: e.copy(bcbc[:, 512 * c:512 * (c + 1)],
                                                ps_att[0][:, :]),
                     waits=[("t3b", r.cnt["t3b"])], inc="s3b")
            ns3b = r.cnt["s3b"]
            for ts in range(NTS):
                sl = ts % 3
                pb2 = ts % 2
                wv = max(0, 12 * (ts - 2))
                for g in range(2):
                    r.op("sp", lambda e, ts=ts, sl=sl, g=g: e.dma_start(
                        out=yT_ld[sl][:, 1024 * g:1024 * (g + 1)].rearrange(
                            "p (s t) -> p s t", t=128),
                        in_=yTS[8 * g:8 * (g + 1), :, 128 * ts:128 * (ts + 1)].rearrange(
                            "s p t -> p s t")),
                        waits=[("p3v", wv)], inc=f"l3a{sl}", inck=16)
                r.op("pl", lambda e, ts=ts, sl=sl: e.dma_start(
                    out=x_ld[sl][:, :], in_=x_in[128 * ts:128 * (ts + 1), :]),
                    waits=[("p3v", wv)], inc=f"l3{sl}", inck=16)
                nl3 = r.cnt[f"l3{sl}"]
                nl3a = r.cnt[f"l3a{sl}"]
                for c in range(4):
                    gi = 4 * ts + c
                    pb = c % 2
                    for s in range(16):
                        w = [("plc", 256), ("s3b", ns3b)]
                        if s == 0:
                            w += [(f"l3{sl}", nl3), (f"l3a{sl}", nl3a),
                                  ("p3v", max(0, 3 * (gi - 1)))]
                        r.op("t", lambda e, pb=pb, sl=sl, s=s, c=c: e.matmul(
                            ps_att[pb][:, :], yT_ld[sl][:, 128 * s:128 * (s + 1)],
                            wc_bf[:, 2048 * s + 512 * c:2048 * s + 512 * (c + 1)],
                            start=(s == 0), stop=(s == 15)),
                            waits=w, inc="t3" if s == 15 else None)
                    r.op("v", lambda e, pb=pb, c=c: e.tensor_tensor(
                        t1_sb[pb][:, :], ps_att[pb][:, :],
                        bcbc[:, 512 * c:512 * (c + 1)], op=ALU.add),
                        waits=[("t3", r.cnt["t3"])], inc="p3v")
                    r.op("v", lambda e, pb=pb, sl=sl, pb2=pb2, c=c: e.tensor_tensor(
                        pre1_st[pb2][:, 512 * c:512 * (c + 1)], t1_sb[pb][:, :],
                        x_ld[sl][:, 512 * c:512 * (c + 1)], op=ALU.add),
                        waits=[("p3v", r.cnt["p3v"]), (f"sp1{pb2}", r.cnt[f"sp1{pb2}"]),
                               ("p3s", max(0, 4 * (ts - 1)))], inc="p3v")
                    r.op("v", lambda e, pb2=pb2, ts=ts, c=c: e.reduce_sum(
                        mp1[:, 4 * ts + c:4 * ts + c + 1],
                        pre1_st[pb2][:, 512 * c:512 * (c + 1)], axis=mybir.AxisListType.X),
                        waits=[("p3v", r.cnt["p3v"])], inc="p3v")
                    r.op("s", lambda e, pb2=pb2, ts=ts, c=c: e.activation(
                        trash[:, :], pre1_st[pb2][:, 512 * c:512 * (c + 1)], AF.Square,
                        accum_out=sq1[:, 4 * ts + c:4 * ts + c + 1]),
                        waits=[("p3v", r.cnt["p3v"]), ("p3s", r.cnt["p3s"])], inc="p3s")
                r.op("s", lambda e, ts=ts, pb2=pb2: e.dma_start(
                    out=pre1S[128 * ts:128 * (ts + 1), :], in_=pre1_st[pb2][:, :]),
                    waits=[("p3v", r.cnt["p3v"])], inc=f"sp1{pb2}", inck=16)
            r.op("sp", lambda e: e.nop(), waits=[("sp10", r.cnt["sp10"]),
                                                 ("sp11", r.cnt["sp11"]),
                                                 ("p3s", r.cnt["p3s"])])
            r.emit_block()

        if upto < 4:
            return nc

        x1T_arena = ExitStack()
        x1T_sb = x1T_arena.enter_context(nc.sbuf_tensor("x1T_sb", [128, CT * T], BF16))
        with ExitStack() as p4:
            pre1_ld = [p4.enter_context(nc.sbuf_tensor(f"pre1_ld{i}", [128, 2048], BF16)) for i in range(2)]
            x1_st = [p4.enter_context(nc.sbuf_tensor(f"x1_st{i}", [128, 2048], BF16)) for i in range(2)]
            up_bf = [p4.enter_context(nc.sbuf_tensor(f"up_bf{i}", [128, 2048], BF16)) for i in range(2)]
            up_s8 = [p4.enter_context(nc.sbuf_tensor(f"up_s8{i}", [128, 2048], FP8)) for i in range(2)]
            hid_st = [p4.enter_context(nc.sbuf_tensor(f"hid_st{i}", [128, T], BF16)) for i in range(2)]
            upsc_sb = p4.enter_context(nc.sbuf_tensor("upsc_sb", [128, 64], F32))
            ps_h = [p4.enter_context(nc.psum_tensor(f"ps_h{i}", [128, 512], F32)) for i in range(2)]
            r = Rec(nc, big, sems)
            for ts in range(NTS):
                r.op("v", lambda e, ts=ts: e.reduce_sum(
                    m1a[:, ts:ts + 1], mp1[:, 4 * ts:4 * (ts + 1)], axis=mybir.AxisListType.X),
                    inc="bv")
                r.op("v", lambda e, ts=ts: e.reduce_sum(
                    v1a[:, ts:ts + 1], sq1[:, 4 * ts:4 * (ts + 1)], axis=mybir.AxisListType.X),
                    inc="bv")
            nred = r.cnt["bv"]
            r.op("v", lambda e: e.tensor_scalar(m1a[:, :], m1a[:, :], 1.0 / C, None, op0=ALU.mult),
                 waits=[("bv", nred)], inc="bv")
            r.op("v", lambda e: e.tensor_scalar(v1a[:, :], v1a[:, :], 1.0 / C, None, op0=ALU.mult),
                 waits=[("bv", r.cnt["bv"])], inc="bv")
            r.op("v", lambda e: e.tensor_tensor(r1a[:, :], m1a[:, :], m1a[:, :], op=ALU.mult),
                 waits=[("bv", r.cnt["bv"])], inc="bv")
            r.op("v", lambda e: e.tensor_tensor(v1a[:, :], v1a[:, :], r1a[:, :], op=ALU.subtract),
                 waits=[("bv", r.cnt["bv"])], inc="bv")
            r.op("s", lambda e: e.activation(r1a[:, :], v1a[:, :], AF.Ln, bias=cbias[:, 3:4]),
                 waits=[("bv", r.cnt["bv"])], inc="bs")
            r.op("s", lambda e: e.activation(r1a[:, :], r1a[:, :], AF.Exp, scale=-0.5),
                 waits=[("bs", r.cnt["bs"])], inc="bs")
            nrs = r.cnt["bs"]
            r.op("sp", lambda e: e.dma_start(out=upsc_sb[:, :], in_=upsc), inc="lu", inck=16)
            for ts in range(NTS):
                sl = ts % 2
                wx1t = r.cnt[f"x1t{sl}"]
                wx1s = r.cnt[f"x1s{sl}"]
                r.op("pl", lambda e, ts=ts, sl=sl: e.dma_start(
                    out=pre1_ld[sl][:, :], in_=pre1S[128 * ts:128 * (ts + 1), :]),
                    waits=[("nv", max(0, 4 * (ts - 1)))], inc=f"l3b{sl}", inck=16)
                nl = r.cnt[f"l3b{sl}"]
                for c in range(4):
                    r.op("v", lambda e, sl=sl, ts=ts, c=c: e.tensor_scalar(
                        x1_st[sl][:, 512 * c:512 * (c + 1)],
                        pre1_ld[sl][:, 512 * c:512 * (c + 1)],
                        m1a[:, ts:ts + 1], r1a[:, ts:ts + 1],
                        op0=ALU.subtract, op1=ALU.mult),
                        waits=[(f"l3b{sl}", nl), ("bs", nrs),
                               (f"x1t{sl}", wx1t), (f"x1s{sl}", wx1s)], inc="nv")
                nv_now = r.cnt["nv"]
                r.op("s", lambda e, ts=ts, sl=sl: e.dma_start(
                    out=x1S[128 * ts:128 * (ts + 1), :], in_=x1_st[sl][:, :]),
                    waits=[("nv", nv_now)], inc=f"x1s{sl}", inck=16)
                r.op("sp", lambda e, ts=ts, sl=sl: e.dma_start(
                    out=x1T_sb[:, :].rearrange("p (k tt) -> p k tt", tt=T)[
                        :, :, 128 * ts:128 * (ts + 1)],
                    in_=x1_st[sl][:, :].rearrange("p (k f) -> p k f", f=128),
                    transpose=True),
                    waits=[("nv", nv_now)], inc=f"x1t{sl}", inck=16)
            for ht in range(HT):
                sl = ht % 2
                for half in range(2):
                    r.op("pl", lambda e, ht=ht, sl=sl, hh=half: e.dma_start(
                        out=up_bf[sl][:, 1024 * hh:1024 * (hh + 1)].rearrange(
                            "p (ct d) -> p ct d", d=128),
                        in_=upT[1024 * hh:1024 * (hh + 1), 128 * ht:128 * (ht + 1)].rearrange(
                            "(ct p) d -> p ct d", p=128)),
                        waits=[("p4s", max(0, ht - 1))], inc=f"plu{sl}", inck=16)
                r.op("s", lambda e, sl=sl: e.activation(up_s8[sl][:, :], up_bf[sl][:, :], AF.Sign),
                     waits=[(f"plu{sl}", 32 * (ht // 2 + 1)),
                            ("p4t", max(0, 8 * (ht - 1)))],
                     inc="p4s")
                for i in range(NCH):
                    for ct in range(CT):
                        w = [("p4s", ht + 1), ("x1t0", 16 * (2 * i + 2)),
                             ("x1t1", 16 * (2 * i + 2))]
                        if ct == 0:
                            w.append(("p4e", max(0, 8 * ht + i - 1)))
                        r.op("t", lambda e, sl=sl, ct=ct, i=i: e.matmul(
                            ps_h[i % 2][:, :], up_s8[sl][:, 128 * ct:128 * (ct + 1)],
                            x1T_sb[:, T * ct + 512 * i:T * ct + 512 * (i + 1)],
                            start=(ct == 0), stop=(ct == CT - 1)),
                            waits=w, inc="p4t" if ct == CT - 1 else None)
                    r.op("s", lambda e, sl=sl, i=i, ht=ht: e.activation(
                        hid_st[sl][:, 512 * i:512 * (i + 1)], ps_h[i % 2][:, :], AF.Relu,
                        scale=upsc_sb[:, ht:ht + 1]),
                        waits=[("p4t", r.cnt["p4t"]), ("lu", 16),
                               (f"p4sp{sl}", r.cnt[f"p4sp{sl}"] if i == 0 else 0)],
                        inc="p4e")
                r.op("sp", lambda e, ht=ht, sl=sl: e.dma_start(
                    out=hidS[ht], in_=hid_st[sl][:, :]),
                    waits=[("p4e", r.cnt["p4e"])], inc=f"p4sp{sl}", inck=16)
            r.op("sp", lambda e: e.nop(), waits=[("p4sp0", r.cnt["p4sp0"]),
                                                 ("p4sp1", r.cnt["p4sp1"]),
                                                 ("x1s0", r.cnt["x1s0"]),
                                                 ("x1s1", r.cnt["x1s1"])])
            r.emit_block()
        x1T_arena.close()

        if upto < 5:

            return nc

        with ExitStack() as p5:
            dsc_sb = p5.enter_context(nc.sbuf_tensor("dsc_sb", [1, C], F32))
            osc_sb = p5.enter_context(nc.sbuf_tensor("osc_sb", [1, 1], F32))
            osc_col = p5.enter_context(nc.sbuf_tensor("osc_col", [128, 1], F32))
            dscbc = p5.enter_context(nc.sbuf_tensor("dscbc", [128, C], F32))
            dn_bf = [p5.enter_context(nc.sbuf_tensor(f"dn_bf{i}", [128, 512], BF16)) for i in range(4)]
            dn_sq = [p5.enter_context(nc.sbuf_tensor(f"dn_sq{i}", [128, HT * 512], FP8)) for i in range(3)]
            hid_ld = [p5.enter_context(nc.sbuf_tensor(f"hid_ld{i}", [128, HT * 256], BF16)) for i in range(2)]
            x1_ld = [p5.enter_context(nc.sbuf_tensor(f"x1_ld{i}", [128, 2048], BF16)) for i in range(2)]
            tmp32 = [p5.enter_context(nc.sbuf_tensor(f"tmp32{i}", [128, 512], F32)) for i in range(2)]
            pre2_st = [p5.enter_context(nc.sbuf_tensor(f"pre2_st{i}", [128, 2048], BF16)) for i in range(2)]
            trash5 = p5.enter_context(nc.sbuf_tensor("trash5", [128, 512], BF16))
            ps_o = [p5.enter_context(nc.psum_tensor(f"ps_o{i}", [128, 512], F32)) for i in range(4)]
            r = Rec(nc, big, sems)
            r.op("sp", lambda e: e.dma_start(out=dsc_sb[:, :], in_=dsc), inc="ld5", inck=16)
            r.op("sp", lambda e: e.dma_start(out=osc_sb[:, :], in_=osc), inc="ld5", inck=16)
            r.op("t", lambda e: e.matmul(ps_o[0][:, 0:1], onescol[:, :], osc_sb[:, :],
                                         start=True, stop=True),
                 waits=[("ld5", 32)], inc="t5d")
            r.op("s", lambda e: e.copy(osc_col[:, :], ps_o[0][:, 0:1]),
                 waits=[("t5d", r.cnt["t5d"])], inc="s5d")
            for c in range(4):
                r.op("t", lambda e, c=c: e.matmul(ps_o[0][:, :], onescol[:, :],
                                                  dsc_sb[0:1, 512 * c:512 * (c + 1)],
                                                  start=True, stop=True),
                     waits=[("s5d", 1), ("d5v", r.cnt["d5v"])], inc="t5d")
                r.op("v", lambda e, c=c: e.tensor_scalar(
                    dscbc[:, 512 * c:512 * (c + 1)], ps_o[0][:, :], osc_col[:, :], None,
                    op0=ALU.mult),
                    waits=[("t5d", r.cnt["t5d"])], inc="d5v")
            nd5v = r.cnt["d5v"]

            sign_i = [0]

            def emit_sign(q, war):
                si = sign_i[0]
                dsl = si % 4
                r.op("pl", lambda e, si=si, dsl=dsl, q=q: e.dma_start(
                    out=dn_bf[dsl][:, :],
                    in_=dnT[128 * (si % HT):128 * (si % HT + 1), 512 * q:512 * (q + 1)]),
                    waits=[("p5s", max(0, si - 3))], inc=f"pld{dsl}", inck=16)
                r.op("s", lambda e, si=si, dsl=dsl, q=q: e.activation(
                    dn_sq[q % 3][:, 512 * (si % HT):512 * (si % HT + 1)],
                    dn_bf[dsl][:, :], AF.Sign),
                    waits=[(f"pld{dsl}", 16 * (si // 4 + 1)), ("p5t", war)],
                    inc="p5s")
                sign_i[0] += 1

            for ht in range(HT):
                emit_sign(0, 0)
            for ht in range(HT):
                emit_sign(1, 0)
            gidx = 0
            for hf in range(2):
                for tc in range(16):
                    hsl = tc % 2
                    wtc = max(0, r.cnt["p5t"] - 4)
                    for g in range(8):
                        eng = "sp" if g % 2 == 0 else "pl"
                        semn = f"l5s{hsl}" if g % 2 == 0 else f"l5p{hsl}"
                        r.op(eng, lambda e, tc=tc, hsl=hsl, g=g: e.dma_start(
                            out=hid_ld[hsl][:, 2048 * g:2048 * (g + 1)].rearrange(
                                "p (h t) -> p h t", t=256),
                            in_=hidS[8 * g:8 * (g + 1), :, 256 * tc:256 * (tc + 1)].rearrange(
                                "h p t -> p h t")),
                            waits=[("p5t", wtc), ("d5v", nd5v)], inc=semn, inck=16)
                    for s_ in range(2):
                        ts = 2 * tc + s_
                        r.op("sp", lambda e, ts=ts, hsl=hsl, s_=s_, hf=hf: e.dma_start(
                            out=x1_ld[hsl][:, 1024 * s_:1024 * (s_ + 1)],
                            in_=x1S[128 * ts:128 * (ts + 1), 1024 * hf:1024 * (hf + 1)]),
                            waits=[("p5v", max(0, r.cnt["p5v"] - 12))], inc=f"l5s{hsl}", inck=16)
                    nl5s = r.cnt[f"l5s{hsl}"]
                    nl5p = r.cnt[f"l5p{hsl}"]
                    wsp2 = r.cnt[f"sp2{hsl}"]
                    wst = max(0, r.cnt["p5st"] - 4)
                    for s_ in range(2):
                        ts = 2 * tc + s_
                        for c in range(2):
                            q = 2 * hf + c
                            pb = gidx % 4
                            for ht in range(HT):
                                w = [("p5s", 64 * q + ht + 1)]
                                if ht == 0:
                                    w += [(f"l5s{hsl}", nl5s), (f"l5p{hsl}", nl5p),
                                          ("p5v", max(0, 3 * (gidx - 3))),
                                          ("d5v", nd5v)]
                                r.op("t", lambda e, pb=pb, hsl=hsl, ht=ht, s_=s_, q=q:
                                     e.matmul(
                                         ps_o[pb][:, :],
                                         hid_ld[hsl][:, 256 * ht + 128 * s_:256 * ht + 128 * (s_ + 1)],
                                         dn_sq[q % 3][:, 512 * ht:512 * (ht + 1)],
                                         start=(ht == 0), stop=(ht == HT - 1)),
                                     waits=w, inc="p5t" if ht == HT - 1 else None)
                            cg = 2 * hf + c
                            r.op("v", lambda e, pb=pb, cg=cg: e.tensor_tensor(
                                tmp32[pb % 2][:, :], ps_o[pb][:, :],
                                dscbc[:, 512 * cg:512 * (cg + 1)], op=ALU.mult),
                                waits=[("p5t", r.cnt["p5t"])], inc="p5v")
                            r.op("v", lambda e, pb=pb, hsl=hsl, s_=s_, c=c: e.tensor_tensor(
                                pre2_st[hsl][:, 1024 * s_ + 512 * c:1024 * s_ + 512 * (c + 1)],
                                tmp32[pb % 2][:, :],
                                x1_ld[hsl][:, 1024 * s_ + 512 * c:1024 * s_ + 512 * (c + 1)],
                                op=ALU.add),
                                waits=[("p5v", r.cnt["p5v"]),
                                       (f"sp2{hsl}", wsp2), ("p5st", wst)], inc="p5v")
                            col = 4 * ts + 2 * hf + c
                            r.op("v", lambda e, hsl=hsl, s_=s_, c=c, col=col: e.reduce_sum(
                                m2p[:, col:col + 1],
                                pre2_st[hsl][:, 1024 * s_ + 512 * c:1024 * s_ + 512 * (c + 1)],
                                axis=mybir.AxisListType.X),
                                waits=[("p5v", r.cnt["p5v"])], inc="p5v")
                            r.op("s", lambda e, hsl=hsl, s_=s_, c=c, col=col: e.activation(
                                trash5[:, :],
                                pre2_st[hsl][:, 1024 * s_ + 512 * c:1024 * s_ + 512 * (c + 1)],
                                AF.Square, accum_out=s2p[:, col:col + 1]),
                                waits=[("p5v", r.cnt["p5v"]), ("p5st", r.cnt["p5st"])],
                                inc="p5st")
                            gidx += 1
                        r.op("s", lambda e, ts=ts, hsl=hsl, s_=s_, hf=hf: e.dma_start(
                            out=pre2S[128 * ts:128 * (ts + 1), 1024 * hf:1024 * (hf + 1)],
                            in_=pre2_st[hsl][:, 1024 * s_:1024 * (s_ + 1)]),
                            waits=[("p5v", r.cnt["p5v"]), ("p5st", r.cnt["p5st"])],
                            inc=f"sp2{hsl}", inck=16)
                    if hf == 0 and tc % 4 == 3:
                        for _ in range(16):
                            emit_sign(2, 0)
                if hf == 0:
                    war3 = r.cnt["p5t"]
                    for ht in range(HT):
                        emit_sign(3, war3)
            r.op("sp", lambda e: e.nop(), waits=[("sp20", r.cnt["sp20"]),
                                                 ("sp21", r.cnt["sp21"])])
            r.emit_block()

        if upto < 6:
            return nc

        with ExitStack() as p6:
            pre2_ld = [p6.enter_context(nc.sbuf_tensor(f"pre2_ld{i}", [128, 2048], BF16)) for i in range(2)]
            out_st = [p6.enter_context(nc.sbuf_tensor(f"out_st{i}", [128, 2048], F32)) for i in range(2)]
            r = Rec(nc, big, sems)
            for ts in range(NTS):
                r.op("v", lambda e, ts=ts: e.reduce_sum(
                    m2a[:, ts:ts + 1], m2p[:, 4 * ts:4 * (ts + 1)], axis=mybir.AxisListType.X),
                    inc="b6")
                r.op("v", lambda e, ts=ts: e.reduce_sum(
                    v2a[:, ts:ts + 1], s2p[:, 4 * ts:4 * (ts + 1)], axis=mybir.AxisListType.X),
                    inc="b6")
            n6 = r.cnt["b6"]
            r.op("v", lambda e: e.tensor_scalar(m2a[:, :], m2a[:, :], 1.0 / C, None, op0=ALU.mult),
                 waits=[("b6", n6)], inc="b6")
            r.op("v", lambda e: e.tensor_scalar(v2a[:, :], v2a[:, :], 1.0 / C, None, op0=ALU.mult),
                 waits=[("b6", r.cnt["b6"])], inc="b6")
            r.op("v", lambda e: e.tensor_tensor(r2a[:, :], m2a[:, :], m2a[:, :], op=ALU.mult),
                 waits=[("b6", r.cnt["b6"])], inc="b6")
            r.op("v", lambda e: e.tensor_tensor(v2a[:, :], v2a[:, :], r2a[:, :], op=ALU.subtract),
                 waits=[("b6", r.cnt["b6"])], inc="b6")
            r.op("s", lambda e: e.activation(r2a[:, :], v2a[:, :], AF.Ln, bias=cbias[:, 3:4]),
                 waits=[("b6", r.cnt["b6"])], inc="s6")
            r.op("s", lambda e: e.activation(r2a[:, :], r2a[:, :], AF.Exp, scale=-0.5),
                 waits=[("s6", r.cnt["s6"])], inc="s6")
            ns6 = r.cnt["s6"]
            for ts in range(NTS):
                sl = ts % 2
                wo6 = r.cnt[f"o6{sl}"]
                r.op("pl", lambda e, ts=ts, sl=sl: e.dma_start(
                    out=pre2_ld[sl][:, :], in_=pre2S[128 * ts:128 * (ts + 1), :]),
                    waits=[("v6", max(0, 4 * (ts - 1)))], inc=f"l6{sl}", inck=16)
                nl6 = r.cnt[f"l6{sl}"]
                for c in range(4):
                    r.op("v", lambda e, sl=sl, ts=ts, c=c: e.tensor_scalar(
                        out_st[sl][:, 512 * c:512 * (c + 1)],
                        pre2_ld[sl][:, 512 * c:512 * (c + 1)],
                        m2a[:, ts:ts + 1], r2a[:, ts:ts + 1],
                        op0=ALU.subtract, op1=ALU.mult),
                        waits=[(f"l6{sl}", nl6), ("s6", ns6),
                               (f"o6{sl}", wo6)], inc="v6")
                r.op("sp", lambda e, ts=ts, sl=sl: e.dma_start(
                    out=out[128 * ts:128 * (ts + 1), :], in_=out_st[sl][:, :]),
                    waits=[("v6", r.cnt["v6"])], inc=f"o6{sl}", inck=16)
            r.op("sp", lambda e: e.nop(), waits=[("o60", r.cnt["o60"]),
                                                 ("o61", r.cnt["o61"])])
            r.emit_block()

    return nc


def make_inputs(hs_b, Wq, Wk, Wv, Wg, Wc, bc, upw, dnw, ups, dns, oscl):
    f = np.float32
    sel = np.zeros((8, 1024), dtype=f)
    for h in range(8):
        sel[h, 128 * h:128 * (h + 1)] = 1.0
    return {
        "x": np.ascontiguousarray(hs_b, dtype=f),
        "xT": np.ascontiguousarray(np.asarray(hs_b, dtype=f).T),
        "Wq": np.ascontiguousarray(Wq, dtype=f),
        "Wk": np.ascontiguousarray(Wk, dtype=f),
        "Wv": np.ascontiguousarray(Wv, dtype=f),
        "WgR": np.ascontiguousarray(
            np.asarray(Wg, dtype=f).reshape(16, 128, 8).transpose(1, 0, 2).reshape(128, 128)),
        "Wc": np.ascontiguousarray(Wc, dtype=f),
        "bc": np.ascontiguousarray(np.asarray(bc, dtype=f).reshape(1, C)),
        "upT": np.ascontiguousarray(np.asarray(upw, dtype=f).T),
        "dnT": np.ascontiguousarray(np.asarray(dnw, dtype=f).T),
        "upsc": np.ascontiguousarray(np.asarray(ups, dtype=f).reshape(64, 128).T),
        "dsc": np.ascontiguousarray(np.asarray(dns, dtype=f).reshape(1, C)),
        "osc": np.ascontiguousarray(np.asarray(oscl, dtype=f).reshape(1, 1)),
        "sel": sel,
        "tri": np.triu(np.ones((128, 128), dtype=f)),
        "ident": np.eye(128, dtype=f),
        "onesc": np.ones((1, 128), dtype=f),
    }


def kernel(L0_attn_hidden_states, L0_attn_Wq, L0_attn_Wk, L0_attn_Wv, L0_attn_Wg,
           L0_attn_Wc, L0_attn_bc, L0_ffn_up_weight, L0_ffn_down_weight,
           L0_ffn_up_scale, L0_ffn_down_scale, L0_ffn_output_scale):
    hs = np.asarray(L0_attn_hidden_states, dtype=np.float32)
    n = 8
    in_maps = [make_inputs(hs[b], L0_attn_Wq, L0_attn_Wk, L0_attn_Wv, L0_attn_Wg,
                           L0_attn_Wc, L0_attn_bc, L0_ffn_up_weight, L0_ffn_down_weight,
                           L0_ffn_up_scale, L0_ffn_down_scale, L0_ffn_output_scale)
               for b in range(n)]
    nc = build(upto=6)
    res = run_bass_kernel_spmd(nc, in_maps, core_ids=list(range(n)))
    return np.stack([np.asarray(res.results[b]["out"], dtype=np.float32)
                     for b in range(n)], axis=0)
